# version 19
# baseline (speedup 1.0000x reference)
import numpy as np
import contextlib
import concourse.bass as bass
import concourse.mybir as mybir
from concourse.bass_utils import run_bass_kernel_spmd


F32 = mybir.dt.float32
BF16 = mybir.dt.bfloat16
AF = mybir.ActivationFunctionType
ALU = mybir.AluOpType
AX = mybir.AxisListType

P = 128
D = 1024
C = D // P
H = 16
MAGIC = float(np.float32(12582912.0))
EPS = 1e-5


class _Sem:
    def __init__(self, name):
        self.name = name
        self.h = None
        self.v = 0


class _Builder:
    ENGINES = ("T", "V", "S", "G", "Y")

    def __init__(self):
        self.streams = {e: [] for e in self.ENGINES}
        self.lastwait = {e: {} for e in self.ENGINES}

    def op(self, eng, fn, wait=(), inc=()):
        w = []
        for sem, val in wait:
            if val > self.lastwait[eng].get(sem.name, -1):
                w.append((sem, val))
                self.lastwait[eng][sem.name] = val
        for sem, amt in inc:
            sem.v += amt

        def run(e, fn=fn, w=tuple(w), inc=tuple(inc)):
            for sem, val in w:
                e.wait_ge(sem.h, val)
            ins = fn(e)
            for sem, amt in inc:
                ins.then_inc(sem.h, amt)
        self.streams[eng].append(run)
        return {sem.name: sem.v for sem, _ in inc}

    def wait(self, eng, sem, val):
        if val > self.lastwait[eng].get(sem.name, -1):
            self.lastwait[eng][sem.name] = val

            def run(e, sem=sem, val=val):
                e.wait_ge(sem.h, val)
            self.streams[eng].append(run)


def build_nc(n_cores, R, z_groups, all_group, debug=False):
    RB = R // P
    RGS = 512
    RG = R // RGS
    NQ = RG

    nc = bass.Bass("TRN2", target_bir_lowering=False)

    xT = nc.dram_tensor("xT", [D, R], F32, kind="ExternalInput")
    x_tm = nc.dram_tensor("x_tm", [R, D], F32, kind="ExternalInput")
    wqT = nc.dram_tensor("wqT", [D, D], F32, kind="ExternalInput")
    wkT = nc.dram_tensor("wkT", [D, D], F32, kind="ExternalInput")
    wvT = nc.dram_tensor("wvT", [D, D], F32, kind="ExternalInput")
    woT = nc.dram_tensor("woT", [D, D], F32, kind="ExternalInput")
    wfT = nc.dram_tensor("wfT", [D, D], F32, kind="ExternalInput")
    bias_row = nc.dram_tensor("bias_row", [1, D], F32, kind="ExternalInput")
    out_tm = nc.dram_tensor("out_tm", [R, D], F32, kind="ExternalOutput")

    z_in = nc.dram_tensor("z_in", [P, 512], F32, kind="Internal")
    z_out = nc.dram_tensor("z_out", [P, 512], F32, kind="Internal")
    mx_in = nc.dram_tensor("mx_in", [1, 8], F32, kind="Internal")
    mx_out = nc.dram_tensor("mx_out", [1, 8], F32, kind="Internal")
    bcst_d = nc.dram_tensor("bcst_d", [8, 512], BF16, kind="Internal")
    pcol_d = nc.dram_tensor("pcol_d", [2, P], F32, kind="Internal")
    sc_d = nc.dram_tensor("sc_d", [1, 8], F32, kind="Internal")
    ident_in = nc.dram_tensor("ident_in", [P, P], F32, kind="ExternalInput")

    stack = contextlib.ExitStack()
    sb = lambda name, shape, dt: stack.enter_context(
        nc.sbuf_tensor(name, shape, dt))
    pst = lambda name, shape, dt: stack.enter_context(
        nc.psum_tensor(name, shape, dt))

    xf = sb("xf", [P, C * R], F32)
    xb = sb("xb", [P, C * R], BF16)
    vq = sb("vq", [P, C * R], BF16)
    wv_sb = sb("wv_sb", [P, C * D], BF16)
    wq_sb = sb("wq_sb", [P, C * D], BF16)
    wk_sb = sb("wk_sb", [P, C * D], BF16)
    qw_sb = wv_sb
    qs = sb("qs", [P, D], BF16)
    ks = sb("ks", [P, D], BF16)
    zs = sb("zs", [P, 512], F32)
    a_sb = sb("a_sb", [P, 512], BF16)
    aT_sb = sb("aT_sb", [P, 256], BF16)
    bdp = sb("bdp", [P, P], BF16)
    identb = sb("identb", [P, P], BF16)
    stq = sb("stq", [P, 4], F32)
    sqs = sb("sqs", [P, 2 * 512], BF16)
    srow = sb("srow", [1, 1536], F32)
    onesb = sb("onesb", [P, 1], BF16)
    bias_sb = sb("bias_sb", [1, D], F32)
    biasa_bf = sb("biasa_bf", [1, D], BF16)
    mr_bb = sb("mr_bb", [P, 2 * RGS], BF16)
    m_bb = mr_bb[:, 0:RGS]
    r_bb = mr_bb[:, RGS:2 * RGS]
    cm_bb = sb("cm_bb", [P, RGS], BF16)
    cr_bb = sb("cr_bb", [P, RGS], BF16)
    scal = sb("scal", [P, 8], F32)
    sc1 = sb("sc1", [1, 16], F32)
    shf = sb("shf", [P, 2], F32)
    prow = sb("prow", [1, 2 * P], F32)
    ones_r = sb("ones_r", [1, P], BF16)
    red_w = sb("red_w", [P, 16], F32)
    cst = sb("cst", [P, 8], F32)
    wfs = sb("wfs", [P, D], F32)
    if R >= 2 * D:
        xtm_buf = [xf[:, (2 * i) * R + R // 2: (2 * i) * R + R // 2 + D]
                   for i in (0, 1)]
        out_buf = [xf[:, (2 * i + 1) * R + R // 2:
                      (2 * i + 1) * R + R // 2 + D] for i in (0, 1)]
    else:
        xtm_t = sb("xtm_t", [P, 2 * D], F32)
        outb_t = sb("outb_t", [P, 2 * D], F32)
        xtm_buf = [xtm_t[:, 0:D], xtm_t[:, D:2 * D]]
        out_buf = [outb_t[:, 0:D], outb_t[:, D:2 * D]]

    pb = [pst("pb%d" % i, [P, 512], F32) for i in range(8)]

    dY = _Sem("dY"); dS = _Sem("dS")
    dXL = [_Sem("dXL%d" % i) for i in range(8)]
    dQR = _Sem("dQR")
    dWV = _Sem("dWV"); dWQ = _Sem("dWQ"); dWK = _Sem("dWK")
    dWF = [_Sem("dWFa"), _Sem("dWFb")]
    dWO = _Sem("dWO"); dBI = _Sem("dBI")
    dZ = _Sem("dZ"); dMX = _Sem("dMX")
    dXT = [_Sem("dXT0"), _Sem("dXT1")]
    dOU = [_Sem("dOU0"), _Sem("dOU1")]
    sT = _Sem("sT"); sV = _Sem("sV"); sS = _Sem("sS")
    sG = _Sem("sG")
    cC = _Sem("cC")
    dBC = _Sem("dBC"); dPC = _Sem("dPC"); dSC = _Sem("dSC")
    dID = _Sem("dID")
    for s in (dY, dS, dXL[0], dXL[1], dXL[2], dXL[3],
              dXL[4], dXL[5], dXL[6], dXL[7], dQR,
              dWV, dWQ, dWK, dWF[0], dWF[1], dWO, dBI,
              dZ, dMX, dXT[0], dXT[1], dOU[0], dOU[1],
              sT, sV, sS, sG, cC, dBC, dPC, dSC, dID):
        s.h = stack.enter_context(nc.semaphore(s.name))

    nc.const_aps.register(F32, MAGIC, cst[:, 1:2])
    nc.const_aps.register(F32, -1.0, cst[:, 4:5])
    nc.const_aps.register(F32, -MAGIC, cst[:, 5:6])

    B = _Builder()
    op = B.op

    def xf_c(c):
        return xf[:, c * R:(c + 1) * R]

    def xb_c(c):
        return xb[:, c * R:(c + 1) * R]

    def vq_c(c):
        return vq[:, c * R:(c + 1) * R]

    def qin_c(c):
        return xf[:, c * R:c * R + R // 2].bitcast(BF16)

    def w_ck(w, dc, lo, n):
        return w[:, dc * D + lo: dc * D + lo + n]

    def w_load(dst, src):
        return lambda e: e.dma_start(
            out=dst[:, :],
            in_=bass.AP(src, 0, [[D, P], [P * D, C], [1, D]]))

    _xq = {0: "Y", 3: "Y", 6: "Y", 1: "S", 4: "S", 7: "S",
           2: "G", 5: "G"}
    for c in (0, 1, 2, 3, 4, 5, 6, 7):
        op(_xq[c], lambda e, c=c: e.dma_start(
            out=xf_c(c), in_=xT[c * P:(c + 1) * P, :]),
            inc=[(dXL[c], 16)])
    xld = {c: (dXL[c], 16) for c in range(C)}

    for j, val in enumerate((EPS, MAGIC, MAGIC + 1.0, 2.0, -1.0, -MAGIC)):
        op("V", lambda e, j=j, val=val: e.memset(cst[:, j:j + 1], val),
           inc=([(sV, 1)] if j == 5 else []))

    op("G", lambda e: e.dma_start(out=identb[:, :], in_=ident_in[:, :]),
       inc=[(dID, 16)])
    g_ident = 16
    op("V", lambda e: e.memset(onesb[:, :], 1.0))
    a = op("V", lambda e: e.memset(ones_r[:, :], 1.0), inc=[(sV, 1)])
    v_ones = sV.v

    op("G", lambda e: e.dma_start(out=bias_sb[:, :], in_=bias_row[:, :]),
       inc=[(dBI, 16)])
    g_bias = (dBI, 16)
    op("G", w_load(wv_sb, wvT), inc=[(dWV, 16)])
    g_wv = (dWV, 16)

    cast_done = {}
    for c in range(C):
        a = op("V", lambda e, c=c: e.tensor_copy(xb_c(c), xf_c(c)),
               wait=[xld[c]], inc=[(sV, 1)])
        cast_done[c] = sV.v
    v_cast_all = sV.v

    corr_done = {}
    g_wq = g_wk = None
    wf_ld = {}
    wf1_red = {}

    def wfh(u):
        return wfs[:, (u % 2) * 512:(u % 2) * 512 + 512]

    def wf_src(u):
        c, half = u // 2, u % 2
        return wfT[c * P:(c + 1) * P, half * 512: half * 512 + 512]

    def emit_wf1_pair2(k):
        for u in (2 * k, 2 * k + 1):
            w = [(sV, wf1_red[u - 2])] if u >= 2 else []
            a = op("Y", lambda e, u=u: e.dma_start(
                out=wfh(u), in_=wf_src(u)),
                wait=w, inc=[(dWF[u % 2], 16)])
            wf_ld[u] = a["dWF%s" % ("ab"[u % 2])]
        for u in (2 * k, 2 * k + 1):
            op("V", lambda e, u=u: e.tensor_reduce(
                red_w[:, u:u + 1], wfh(u),
                AX.X, ALU.add, apply_absolute_value=True),
                wait=[(dWF[u % 2], wf_ld[u])], inc=[(sV, 1)])
            wf1_red[u] = sV.v
    quad_gbco = {}
    sq_done = {}
    sq_used = {}
    stat_done = {}
    srow_rd = {}

    def emit_stats_mms(q):
        for c in range(C):
            ping = (q * C + c) % 2
            vw = [xld[c], (sV, cast_done[c])]
            if (q * C + c) >= 2:
                vw.append((sT, sq_used[(q * C + c) - 2]))
            op("S", lambda e, c=c, q=q, ping=ping: e.activation(
                sqs[:, ping * 512:(ping + 1) * 512],
                xb_c(c)[:, q * RGS: q * RGS + RGS], AF.Square),
                wait=vw, inc=[(sS, 1)])
            sq_done[q * C + c] = sS.v
            tw = [(sS, sq_done[q * C + c]), (sV, v_ones)]
            if c == 0 and q >= 1:
                tw.append((sV, srow_rd[q - 1]))
            op("T", lambda e, c=c, q=q: e.matmul(
                pb[6][0:1, 0:512], onesb[:, :],
                xb_c(c)[:, q * RGS: q * RGS + RGS],
                start=(c == 0), stop=(c == C - 1)),
                wait=tw)
            op("T", lambda e, c=c, ping=ping: e.matmul(
                pb[7][0:1, 0:512], onesb[:, :],
                sqs[:, ping * 512:(ping + 1) * 512],
                start=(c == 0), stop=(c == C - 1)),
                inc=[(sT, 1)])
            sq_used[q * C + c] = sT.v
        stat_done[q] = sT.v

    def emit_stats_post(q):
        t_stat = stat_done[q]
        op("V", lambda e: e.tensor_scalar(
            srow[0:1, 0:512], pb[6][0:1, 0:512], 1.0 / D, None, ALU.mult),
            wait=[(sT, t_stat)], inc=[(sV, 1)])
        op("V", lambda e: e.tensor_scalar(
            srow[0:1, 512:1024], pb[7][0:1, 0:512], 1.0 / D, None,
            ALU.mult), inc=[(sV, 1)])
        srow_rd[q] = sV.v
        B.wait("V", sV, sV.v)
        op("V", lambda e: e.tensor_tensor(
            srow[0:1, 1024:1536], srow[0:1, 0:512], srow[0:1, 0:512],
            ALU.mult), inc=[(sV, 1)])
        B.wait("V", sV, sV.v)
        op("V", lambda e: e.tensor_tensor(
            srow[0:1, 512:1024], srow[0:1, 512:1024], srow[0:1, 1024:1536],
            ALU.subtract), inc=[(sV, 1)])
        v_var = sV.v
        op("S", lambda e: e.activation(
            srow[0:1, 1024:1536], srow[0:1, 512:1024], AF.Ln,
            bias=cst[0:1, 0:1]), wait=[(sV, v_var)], inc=[(sS, 1)])
        B.wait("S", sS, sS.v)
        a = op("S", lambda e: e.activation(
            srow[0:1, 1024:1536], srow[0:1, 1024:1536], AF.Exp,
            scale=-0.5), inc=[(sS, 1)])
        s_rstd = sS.v
        gw = [(sS, s_rstd)]
        if q >= 1:
            gw.append(corr_done[q - 1])
        a = op("G", lambda e, q=q: e.dma_start(
            out=bcst_d[q:q + 1, :], in_=srow[0:1, 0:512]),
            wait=gw, inc=[(dBC, 16)])
        a = op("G", lambda e, q=q: e.dma_start(
            out=bcst_d[4 + q:5 + q, :], in_=srow[0:1, 1024:1536]),
            inc=[(dBC, 16)])
        quad_gbco[q] = a["dBC"]
        a = op("G", lambda e, q=q: e.dma_start(
            out=cm_bb[:, :],
            in_=bass.AP(bcst_d, q * 512, [[0, P], [1, 512]])),
            wait=[(dBC, quad_gbco[q])], inc=[(dBC, 16)])
        a = op("G", lambda e, q=q: e.dma_start(
            out=cr_bb[:, :],
            in_=bass.AP(bcst_d, (4 + q) * 512, [[0, P], [1, 512]])),
            inc=[(dBC, 16)])
        g_corr = a["dBC"]
        for c in range(C):
            op("V", lambda e, c=c, q=q: e.tensor_tensor(
                xb_c(c)[:, q * RGS: q * RGS + RGS],
                xb_c(c)[:, q * RGS: q * RGS + RGS], cm_bb[:, :],
                ALU.subtract),
                wait=[(dBC, g_corr), (sV, cast_done[c])], inc=[(sV, 1)])
        v_sub = sV.v
        for c in range(C):
            op("V", lambda e, c=c, q=q: e.tensor_tensor(
                xb_c(c)[:, q * RGS: q * RGS + RGS],
                xb_c(c)[:, q * RGS: q * RGS + RGS], cr_bb[:, :],
                ALU.mult),
                wait=([(sV, v_sub)] if c == 0 else []), inc=[(sV, 1)])
        corr_done[q] = (sV, sV.v)

    emit_stats_mms(0)
    emit_stats_post(0)
    if NQ > 1:
        emit_stats_mms(1)
        emit_stats_post(1)
        emit_wf1_pair2(0)
    op("G", w_load(wq_sb, wqT), inc=[(dWQ, 16)])
    g_wq = (dWQ, 16)
    op("G", w_load(wk_sb, wkT), inc=[(dWK, 16)])
    g_wk = (dWK, 16)
    vp_evac = {}
    vp_bank_last = {}
    for rg in range(RG):
        q = rg

        for ocb in range(C):
            bank = pb[ocb % 6]
            w0 = [corr_done[q], g_wv]
            if (ocb % 6) in vp_bank_last:
                w0.append((sS, vp_bank_last[ocb % 6]))
            for dc in range(C):
                op("T", lambda e, rg=rg, ocb=ocb, dc=dc, bank=bank: e.matmul(
                    bank[:, 0:512],
                    wv_sb[:, dc * D + ocb * P: dc * D + (ocb + 1) * P],
                    xb_c(dc)[:, rg * RGS: rg * RGS + RGS],
                    start=(dc == 0), stop=(dc == C - 1)),
                    wait=(w0 if dc == 0 else []),
                    inc=([(sT, 1)] if dc == C - 1 else []))
            t_g = sT.v
            sw = [(sT, t_g)]

            a = op("S", lambda e, rg=rg, ocb=ocb, bank=bank: e.copy(
                vq_c(ocb)[:, rg * RGS: rg * RGS + RGS], bank[:, 0:512]),
                wait=sw, inc=[(sS, 1)])
            vp_evac[(rg, ocb)] = sS.v
            vp_bank_last[ocb % 6] = sS.v
        if rg + 2 < NQ:
            emit_stats_mms(rg + 2)
            emit_stats_post(rg + 2)
            emit_wf1_pair2(rg + 1)
    vp_last = sS.v
    t_vp_last = sT.v

    nrm_state = {"done": {}, "amax_parts": 0}

    def emit_norm_quad(q, part):
        if part == 0:
            base = 4 * NQ * 16 + 32 * q
            a = op("G", lambda e, q=q: e.dma_start(
                out=m_bb[:, :],
                in_=bass.AP(bcst_d, q * 512, [[0, P], [1, 512]])),
                wait=[(dBC, base),
                      (sV, nrm_state["done"].get(q - 1, 0))],
                inc=[(dBC, 16)])
            a = op("G", lambda e, q=q: e.dma_start(
                out=r_bb[:, :],
                in_=bass.AP(bcst_d, (4 + q) * 512, [[0, P], [1, 512]])),
                inc=[(dBC, 16)])
            g_n = a["dBC"]
            for c in range(C):
                op("V", lambda e, c=c, q=q: e.tensor_tensor(
                    xf_c(c)[:, q * RGS: q * RGS + RGS],
                    xf_c(c)[:, q * RGS: q * RGS + RGS], m_bb[:, :],
                    ALU.subtract),
                    wait=[(dBC, g_n), (sV, cast_done[c])], inc=[(sV, 1)])
            B.wait("V", sV, sV.v)
        else:
            for c in range(C):
                op("V", lambda e, c=c, q=q: e.tensor_tensor(
                    xf_c(c)[:, q * RGS: q * RGS + RGS],
                    xf_c(c)[:, q * RGS: q * RGS + RGS], r_bb[:, :],
                    ALU.mult), inc=[(sV, 1)])
            nrm_state["done"][q] = sV.v
        return sV.v

    def emit_absmax_part(q, h):
        for c in range(4 * h, 4 * h + 4):
            op("V", lambda e, c=c, q=q: e.tensor_reduce(
                stq[:, 0:1], xf_c(c)[:, q * RGS: q * RGS + RGS],
                AX.X, ALU.max, apply_absolute_value=True),
                wait=[(sV, nrm_state["done"][q])], inc=[(sV, 1)])
            B.wait("V", sV, sV.v)
            if nrm_state["amax_parts"] == 0:
                op("V", lambda e: e.tensor_copy(stq[:, 1:2], stq[:, 0:1]),
                   inc=[(sV, 1)])
            else:
                op("V", lambda e: e.tensor_max(stq[:, 1:2], stq[:, 1:2],
                                               stq[:, 0:1]), inc=[(sV, 1)])
            nrm_state["amax_parts"] += 1
            B.wait("V", sV, sV.v)
        return sV.v

    mxs = {"g_mx": None, "staged": False}

    def _mx1():
        emit_mx_stage1()
        mxs["staged"] = True

    def _mx2():
        mxs["g_mx"] = emit_mx_stage2()

    slot_items = []
    for q in range(NQ):
        slot_items.append((5.5, lambda q=q: emit_norm_quad(q, 0), False))
        slot_items.append((5.5, lambda q=q: emit_norm_quad(q, 1), False))
        for h in range(2):
            slot_items.append(
                (3.3, lambda q=q, h=h: emit_absmax_part(q, h), False))
    slot_items.append((2.5, _mx1, True))
    slot_items.append((2.5, _mx2, True))
    slot_cursor = 0

    tail_budget = 42.0
    tail_used = 0.0

    def _tail_ok():
        return (slot_cursor < len(slot_items)
                and not slot_items[slot_cursor][2]
                and tail_used + slot_items[slot_cursor][0] <= tail_budget)
    while len(wf1_red) < 2 * C or _tail_ok():
        if len(wf1_red) < 2 * C:
            emit_wf1_pair2(len(wf1_red) // 2)
        if _tail_ok():
            tail_used += slot_items[slot_cursor][0]
            slot_items[slot_cursor][1]()
            slot_cursor += 1
    B.wait("V", sV, sV.v)
    op("V", lambda e: e.tensor_reduce(shf[:, 0:1], red_w[:, 0:16], AX.X,
                                      ALU.add), inc=[(sV, 1)])
    v_wtot = sV.v
    a = op("G", lambda e: e.dma_start(out=pcol_d[1:2, :], in_=shf[:, 0:1]),
           wait=[(sV, v_wtot)], inc=[(dPC, 16)])
    a = op("G", lambda e: e.dma_start(out=prow[0:1, P:2 * P],
                                      in_=pcol_d[1:2, :]),
           wait=[(dPC, 16)], inc=[(dPC, 16)])
    def emit_winv_stageB():
        a = op("V", lambda e: e.tensor_reduce(
            sc1[0:1, 1:2], prow[0:1, P:2 * P], AX.X, ALU.add),
            wait=[(dPC, 32)], inc=[(sV, 1)])
        B.wait("V", sV, sV.v)
        a = op("V", lambda e: e.tensor_scalar(
            sc1[0:1, 2:3], sc1[0:1, 1:2], 1.0 / (D * D), None, ALU.mult),
            inc=[(sV, 1)])
        B.wait("V", sV, sV.v)
        op("V", lambda e: e.reciprocal(sc1[0:1, 3:4], sc1[0:1, 2:3]),
           inc=[(sV, 1)])
        v_winv = sV.v
        a = op("Y", lambda e: e.dma_start(out=sc_d[0:1, 2:3],
                                          in_=sc1[0:1, 3:4]),
               wait=[(sV, v_winv)], inc=[(dSC, 16)])
        a = op("Y", lambda e: e.dma_start(
            out=scal[:, 2:3], in_=bass.AP(sc_d, 2, [[0, P], [1, 1]])),
            wait=[(dSC, 16)], inc=[(dSC, 16)])
    g_winv = 32

    def emit_mx_stage1():
        op("V", lambda e: e.tensor_copy(shf[:, 0:1], stq[:, 1:2]),
           wait=[(dPC, 32)], inc=[(sV, 1)])
        a = op("G", lambda e: e.dma_start(out=pcol_d[0:1, :],
                                          in_=shf[:, 0:1]),
               wait=[(sV, sV.v)], inc=[(dPC, 16)])
        a = op("G", lambda e: e.dma_start(out=prow[0:1, 0:P],
                                          in_=pcol_d[0:1, :]),
               wait=[(dPC, 48)], inc=[(dPC, 16)])

    def emit_mx_stage2():
        a = op("V", lambda e: e.tensor_reduce(
            shf[0:1, 1:2], prow[0:1, 0:P], AX.X, ALU.max),
            wait=[(dPC, 64)], inc=[(sV, 1)])
        B.wait("V", sV, sV.v)
        op("V", lambda e: e.memset(sc1[0:1, 8:16], 0.0), inc=[(sV, 1)])
        B.wait("V", sV, sV.v)
        a = op("V", lambda e: e.tensor_scalar(
            sc1[0:1, 8:16], sc1[0:1, 8:16], shf[0:1, 1:2], None, ALU.add),
            inc=[(sV, 1)])
        v_mx8 = sV.v
        a = op("G", lambda e: e.dma_start(
            out=mx_in[0:1, 0:8], in_=sc1[0:1, 8:16]),
            wait=[(sV, v_mx8)], inc=[(dMX, 16)])
        a = op("G", lambda e: e.collective_compute(
            "AllReduce", ALU.max, replica_groups=all_group,
            ins=[mx_in.ap().opt()], outs=[mx_out.ap().opt()]),
            wait=[(dMX, 16)], inc=[(cC, 1)])
        a = op("G", lambda e: e.dma_start(out=sc1[0:1, 0:1],
                                          in_=mx_out[0:1, 0:1]),
               wait=[(cC, cC.v)], inc=[(dMX, 16)])
        return (dMX, 32)

    wf2_ld = {}
    wf2_use = {}

    def emit_wf2_load(u):
        w = [(sV, v_wtot)] if u < 2 else [(sS, wf2_use[u - 2])]
        a = op("Y", lambda e, u=u: e.dma_start(
            out=wfh(u), in_=wf_src(u)), wait=w, inc=[(dWF[u % 2], 16)])
        wf2_ld[u] = a["dWF%s" % ("ab"[u % 2])]

    def emit_qw_slot(c, half):
        u = 2 * c + half
        if u not in wf2_ld:
            emit_wf2_load(u)
        if u + 1 not in wf2_ld and u + 1 < 2 * C:
            emit_wf2_load(u + 1)
        w = [(dSC, g_winv), (sS, sS.v), (dWF[u % 2], wf2_ld[u]),
             (sT, t_vp_last)]
        op("S", lambda e, u=u: e.activation(
            pb[6][:, 0:512],
            wfh(u),
            AF.Copy, bias=MAGIC, scale=scal[:, 2:3]),
            wait=w + [(sV, v_wtot)], inc=[(sS, 1)])
        wf2_use[u] = sS.v
        B.wait("S", sS, sS.v)
        op("S", lambda e: e.activation(pb[7][:, 0:512], pb[6][:, 0:512],
                                       AF.Relu, bias=cst[:, 2:3],
                                       scale=-1.0), inc=[(sS, 1)])
        B.wait("S", sS, sS.v)
        op("S", lambda e: e.activation(pb[6][:, 0:512], pb[7][:, 0:512],
                                       AF.Relu, bias=cst[:, 3:4], scale=-1.0),
           inc=[(sS, 1)])
        B.wait("S", sS, sS.v)
        op("S", lambda e, c=c, half=half: e.activation(
            qw_sb[:, c * D + half * 512: c * D + half * 512 + 512],
            pb[6][:, 0:512], AF.Copy, bias=-1.0), inc=[(sS, 1)])
        return sS.v

    def emit_quant_chunk(c, g_scal_val, eng):
        nd = nrm_state["done"][NQ - 1]
        if eng == "S":
            op("S", lambda e, c=c: e.activation(
                xf_c(c), xf_c(c), AF.Copy, bias=MAGIC, scale=scal[:, 0:1]),
                wait=[(sV, nd), (dSC, g_scal_val)], inc=[(sS, 1)])
            s1 = (sS, sS.v)
        else:
            op("V", lambda e, c=c: e.tensor_scalar(
                xf_c(c), xf_c(c), scal[:, 0:1], MAGIC, ALU.mult, ALU.add),
                wait=[(sV, nd), (dSC, g_scal_val)], inc=[(sV, 1)])
            s1 = (sV, sV.v)
        if c % 2 == 0:
            op("S", lambda e, c=c: e.activation(
                qin_c(c), xf_c(c), AF.Copy, bias=-MAGIC),
                wait=[s1], inc=[(sS, 1)])
            return ("S", (sS, sS.v))
        op("G", lambda e, c=c: e.tensor_scalar(
            qin_c(c), xf_c(c), -MAGIC, None, ALU.add),
            wait=[s1], inc=[(sG, 1)])
        return ("G", (sG, sG.v))

    def emit_sc1_chain(g_mx_w):
        a = op("V", lambda e: e.tensor_scalar(
            sc1[0:1, 4:5], sc1[0:1, 0:1], 1.0 / 127.0, None, ALU.mult),
            wait=[g_mx_w], inc=[(sV, 1)])
        B.wait("V", sV, sV.v)
        op("V", lambda e: e.reciprocal(sc1[0:1, 5:6], sc1[0:1, 4:5]),
           inc=[(sV, 1)])
        B.wait("V", sV, sV.v)
        op("V", lambda e: e.tensor_tensor(
            sc1[0:1, 6:7], sc1[0:1, 2:3], sc1[0:1, 4:5], ALU.mult),
            inc=[(sV, 1)])
        v_alpha = sV.v
        B.wait("V", sV, sV.v)
        op("V", lambda e: e.reciprocal(sc1[0:1, 7:8], sc1[0:1, 6:7]),
           wait=[g_bias], inc=[(sV, 1)])
        B.wait("V", sV, sV.v)
        op("V", lambda e: e.tensor_scalar(
            biasa_bf[0:1, :], bias_sb[0:1, :], sc1[0:1, 7:8], None,
            ALU.mult), inc=[(sV, 1)])
        vb = sV.v
        a = op("Y", lambda e: e.dma_start(
            out=sc_d[0:1, 0:1], in_=sc1[0:1, 5:6]),
            wait=[(sV, v_alpha), (dSC, g_winv)], inc=[(dSC, 16)])
        a = op("Y", lambda e: e.dma_start(
            out=sc_d[0:1, 1:2], in_=sc1[0:1, 6:7]),
            inc=[(dSC, 16)])
        a = op("Y", lambda e: e.dma_start(
            out=sc_d[0:1, 3:4], in_=sc1[0:1, 7:8]),
            wait=[(sV, vb)], inc=[(dSC, 16)])
        a = op("Y", lambda e: e.dma_start(
            out=scal[:, 0:1], in_=bass.AP(sc_d, 0, [[0, P], [1, 1]])),
            wait=[(dSC, 80)], inc=[(dSC, 16)])
        a = op("Y", lambda e: e.dma_start(
            out=scal[:, 1:2], in_=bass.AP(sc_d, 1, [[0, P], [1, 1]])),
            inc=[(dSC, 16)])
        a = op("Y", lambda e: e.dma_start(
            out=scal[:, 3:4], in_=bass.AP(sc_d, 3, [[0, P], [1, 1]])),
            inc=[(dSC, 16)])
        return vb, 128

    qw_slots = [(c, half) for c in range(C) for half in range(2)]
    qw_cursor = 0
    quant_cursor = 0
    s_qw = None
    quant_done = {}

    qk_done = {}
    qevac = {}
    kevac = {}
    zfirst = {4: True, 5: True}
    for rb in range(RB):
        q = rb // 4
        w0 = [corr_done[q], g_wq, g_wk]
        if rb == 0:
            w0.append((sS, vp_last))
        else:
            w0.append((sS, qevac[rb - 1]))
            w0.append(kevac[rb - 1])
        first = True
        for dc in range(C):
            for half in range(2):
                op("T", lambda e, rb=rb, dc=dc, half=half: e.matmul(
                    pb[half][:, 0:512],
                    xb_c(dc)[:, rb * P:(rb + 1) * P],
                    w_ck(wq_sb, dc, half * 512, 512),
                    start=(dc == 0), stop=(dc == C - 1)),
                    wait=(w0 if first else []))
                first = False
                op("T", lambda e, rb=rb, dc=dc, half=half: e.matmul(
                    pb[2 + half][:, 0:512],
                    xb_c(dc)[:, rb * P:(rb + 1) * P],
                    w_ck(wk_sb, dc, half * 512, 512),
                    start=(dc == 0), stop=(dc == C - 1)),
                    inc=([(sT, 1)] if dc == C - 1 and half == 1 else []))
        qk_done[rb] = sT.v
        sw = [(sT, qk_done[rb])]
        op("S", lambda e, rb=rb: e.copy(
            qs[:, 0:512], pb[0][:, 0:512]), wait=sw)
        op("S", lambda e, rb=rb: e.copy(
            ks[:, 0:512], pb[2][:, 0:512]), inc=[(sS, 1)])
        qevac[rb] = sS.v
        if rb < 6:
            op("S", lambda e, rb=rb: e.copy(
                qs[:, 512:1024], pb[1][:, 0:512]))
            op("S", lambda e, rb=rb: e.copy(
                ks[:, 512:1024], pb[3][:, 0:512]), inc=[(sS, 1)])
            qevac[rb] = sS.v
            kevac[rb] = (sS, sS.v)
        else:
            op("V", lambda e, rb=rb: e.tensor_copy(
                qs[:, 512:1024], pb[1][:, 0:512]), wait=sw)
            op("V", lambda e, rb=rb: e.tensor_copy(
                ks[:, 512:1024], pb[3][:, 0:512]), inc=[(sV, 1)])
            kevac[rb] = (sV, sV.v)
        for g in range(8):
            bank = pb[4 + g // 4]
            lo = (g % 4) * P
            st = zfirst[4 + g // 4]
            zfirst[4 + g // 4] = False
            zw = []
            if g == 0:
                zw = [(sS, qevac[rb]), kevac[rb]]
            op("T", lambda e, g=g, bank=bank, lo=lo, st=st, rb=rb: e.matmul(
                bank[:, lo:lo + P],
                ks[:, g * P:(g + 1) * P], qs[:, g * P:(g + 1) * P],
                start=st, stop=(rb == RB - 1 and (g % 4) == 3)),
                wait=zw, inc=([(sT, 1)] if g == 7 else []))
        used = 0.0
        if rb == 0:
            emit_winv_stageB()
            used += 2.0
        while slot_cursor < len(slot_items) and (
                used == 0.0
                or used + slot_items[slot_cursor][0] <= 8.0):
            used += slot_items[slot_cursor][0]
            slot_items[slot_cursor][1]()
            slot_cursor += 1
        if 2 <= rb < 10 and qw_cursor < len(qw_slots):
            for _ in range(2):
                if qw_cursor < len(qw_slots):
                    cc_, hh_ = qw_slots[qw_cursor]
                    s_qw = emit_qw_slot(cc_, hh_)
                    qw_cursor += 1
    t_zdone = sT.v
    while slot_cursor < len(slot_items):
        slot_items[slot_cursor][1]()
        slot_cursor += 1
    while qw_cursor < len(qw_slots):
        cc_, hh_ = qw_slots[qw_cursor]
        s_qw = emit_qw_slot(cc_, hh_)
        qw_cursor += 1

    op("G", w_load(wk_sb, woT), wait=[(sT, qk_done[RB - 1])],
       inc=[(dWO, 16)])
    g_wo = (dWO, 16)

    v_biasa, g_scal = emit_sc1_chain(mxs["g_mx"])

    for g in range(8):
        bank = pb[4 + g // 4]
        lo = (g % 4) * P
        op("V", lambda e, g=g, bank=bank, lo=lo: e.tensor_copy(
            zs[0:64, g * 64:(g + 1) * 64], bank[0:64, lo:lo + 64]),
            wait=([(sT, t_zdone)] if g == 0 else []), inc=[(sV, 1)])
        op("V", lambda e, g=g, bank=bank, lo=lo: e.tensor_copy(
            zs[64:128, g * 64:(g + 1) * 64],
            bank[64:128, lo + 64:lo + 128]), inc=[(sV, 1)])
    v_zevac = sV.v
    a = op("G", lambda e: e.dma_start(out=z_in[:, :], in_=zs[:, :]),
           wait=[(sV, v_zevac)], inc=[(dZ, 16)])
    a = op("G", lambda e: e.collective_compute(
        "AllReduce", ALU.add, replica_groups=z_groups,
        ins=[z_in.ap().opt()], outs=[z_out.ap().opt()]),
        wait=[(dZ, 16)], inc=[(cC, 1)])
    c_z = cC.v
    while quant_cursor < C:
        _, quant_done[quant_cursor] = emit_quant_chunk(
            quant_cursor, g_scal, "V" if quant_cursor % 2 == 0 else "S")
        quant_cursor += 1
    s_quant = sS.v
    v_quant = sV.v
    g_quant = sG.v
    a = op("G", lambda e: e.dma_start(out=zs[:, :], in_=z_out[:, :]),
           wait=[(cC, c_z)], inc=[(dZ, 16)])
    g_zr = (dZ, 32)

    for g in range(8):
        op("V", lambda e, g=g: e.tensor_reduce(
            red_w[:, g:g + 1], zs[:, g * 64:(g + 1) * 64], AX.X, ALU.max),
            wait=([g_zr] if g == 0 else []), inc=[(sV, 1)])
    B.wait("V", sV, sV.v)
    for g in range(8):
        op("V", lambda e, g=g: e.tensor_scalar(
            red_w[:, g:g + 1], red_w[:, g:g + 1], -0.125, None, ALU.mult),
            inc=[(sV, 1)])
    v_zmax = sV.v
    for g in range(8):
        op("S", lambda e, g=g: e.activation(
            zs[:, g * 64:(g + 1) * 64], zs[:, g * 64:(g + 1) * 64],
            AF.Exp, bias=red_w[:, g:g + 1], scale=0.125),
            wait=([(sV, v_zmax)] if g == 0 else []), inc=[(sS, 1)])
    s_exp = sS.v
    for g in range(8):
        op("V", lambda e, g=g: e.tensor_reduce(
            red_w[:, g:g + 1], zs[:, g * 64:(g + 1) * 64], AX.X, ALU.add),
            wait=([(sS, s_exp)] if g == 0 else []), inc=[(sV, 1)])
    B.wait("V", sV, sV.v)
    op("V", lambda e: e.reciprocal(red_w[:, 0:8], red_w[:, 0:8]),
       inc=[(sV, 1)])
    B.wait("V", sV, sV.v)
    for g in range(8):
        op("V", lambda e, g=g: e.tensor_scalar(
            a_sb[:, g * 64:(g + 1) * 64], zs[:, g * 64:(g + 1) * 64],
            red_w[:, g:g + 1], None, ALU.mult), inc=[(sV, 1)])
    v_soft = sV.v

    ep_done = {}
    xtm2_ld = {}
    out_st = {}

    def emit_ffn(rb, extra=()):
        fb = [pb[4 + 2 * (rb % 2)], pb[4 + 2 * (rb % 2) + 1]]
        wt = [(sS, s_qw), (sV, v_biasa),
              (sV, v_ones), (dSC, g_scal)] + list(extra)
        if rb >= 2:
            wt.append((sV, ep_done[rb - 2]))
        for half in range(2):
            op("T", lambda e, half=half, fb=fb: e.matmul(
                fb[half][:, 0:512], ones_r[0:1, :],
                biasa_bf[0:1, half * 512: half * 512 + 512],
                start=True, stop=False), wait=(wt if half == 0 else []))
        for dc in range(C):
            for half in range(2):
                op("T", lambda e, rb=rb, dc=dc, half=half, fb=fb: e.matmul(
                    fb[half][:, 0:512],
                    qin_c(dc)[:, rb * P:(rb + 1) * P],
                    w_ck(qw_sb, dc, half * 512, 512),
                    start=False, stop=False),
                    wait=([quant_done[dc]] if half == 0 else []))
        return fb


    ffn_pre = {0: emit_ffn(0, extra=((sV, v_zevac),)),
               1: emit_ffn(1, extra=((sV, v_zevac),))}

    wtil_evac = {}
    bd_built = {}
    tr_done = {}
    abd_done = {}
    wtilmm_done = {}
    for mc in range(C):
        g = mc
        vw = [(sV, v_soft)]
        if mc >= 1:
            vw.append((sT, tr_done[mc - 1]))
        op("V", lambda e: e.memset(bdp[:, :], 0.0), wait=vw, inc=[(sV, 1)])
        B.wait("V", sV, sV.v)
        op("V", lambda e, g=g: e.tensor_copy(
            bdp[0:64, 0:64], a_sb[0:64, g * 64:(g + 1) * 64]), inc=[(sV, 1)])
        op("V", lambda e, g=g: e.tensor_copy(
            bdp[64:128, 64:128], a_sb[64:128, g * 64:(g + 1) * 64]),
            inc=[(sV, 1)])
        bd_built[mc] = sV.v
        tb = pb[mc % 2]
        tw = [(sV, bd_built[mc]), (dID, g_ident)]
        if mc == 0:
            tw.append((sS, qevac[RB - 1]))
        if mc >= 2:
            tw.append((sS, abd_done[mc - 2]))
        a = op("T", lambda e, tb=tb: e.transpose(
            tb[:, 0:64].bitcast(BF16), bdp[:, :], identb[:, :]),
            wait=tw, inc=[(sT, 1)])
        tr_done[mc] = sT.v
        a = op("S", lambda e, mc=mc, tb=tb: e.copy(
            aT_sb[:, (mc % 2) * P:(mc % 2) * P + P],
            tb[:, 0:64].bitcast(BF16)),
            wait=[(sT, tr_done[mc])] +
                 ([(sT, wtilmm_done[mc - 1])] if mc >= 1 else []),
            inc=[(sS, 1)])
        abd_done[mc] = sS.v
        s_abd = sS.v
        for half in range(2):
            wt = [(sS, s_abd), g_wo]
            if mc >= 1:
                wt.append((sS, wtil_evac[mc - 1]))
            op("T", lambda e, mc=mc, half=half: e.matmul(
                pb[2 + half][:, 0:512],
                aT_sb[:, (mc % 2) * P:(mc % 2) * P + P],
                wk_sb[:, mc * D + half * 512: mc * D + half * 512 + 512],
                start=True, stop=True), wait=wt,
                inc=([(sT, 1)] if half == 1 else []))
        wtilmm_done[mc] = sT.v
        for half in range(2):
            op("S", lambda e, mc=mc, half=half: e.activation(
                wk_sb[:, mc * D + half * 512: mc * D + half * 512 + 512],
                pb[2 + half][:, 0:512], AF.Copy, scale=scal[:, 3:4]),
                wait=[(sT, wtilmm_done[mc]), (dSC, g_scal)],
                inc=([(sS, 1)] if half == 1 else []))
        wtil_evac[mc] = sS.v
    s_wtil = sS.v


    for rb in range(RB):
        w = [(sS, s_quant), (sV, v_quant), (sG, g_quant)]
        if rb >= 2:
            w.append((sV, ep_done[rb - 2]))
        a = op("Y", lambda e, rb=rb: e.dma_start(
            out=xtm_buf[rb % 2], in_=x_tm[rb * P:(rb + 1) * P, :]),
            wait=w, inc=[(dXT[rb % 2], 16)])
        xtm2_ld[rb] = a["dXT%d" % (rb % 2)]

        if rb < 2:
            ab = ffn_pre[rb]
        else:
            ab = emit_ffn(rb)
        for mc in range(C):
            for half in range(2):
                op("T", lambda e, rb=rb, mc=mc, half=half, ab=ab: e.matmul(
                    ab[half][:, 0:512],
                    vq_c(mc)[:, rb * P:(rb + 1) * P],
                    w_ck(wk_sb, mc, half * 512, 512),
                    start=False, stop=(mc == C - 1)),
                    wait=([(sS, s_wtil)] if mc == 0 and half == 0 else []),
                    inc=([(sT, 1)] if mc == C - 1 and half == 1 else []))
        t_rb = sT.v
        for half in range(2):
            op("V", lambda e, rb=rb, half=half, ab=ab: e.scalar_tensor_tensor(
                out_buf[rb % 2][:, half * 512: half * 512 + 512],
                ab[half][:, 0:512], scal[:, 1:2],
                xtm_buf[rb % 2][:, half * 512: half * 512 + 512],
                ALU.mult, ALU.add),
                wait=[(sT, t_rb), (dXT[rb % 2], xtm2_ld[rb]),
                      (dSC, g_scal)] +
                     ([(dOU[rb % 2], out_st[rb - 2])] if rb >= 2 else []),
                inc=[(sV, 1)])
        ep_done[rb] = sV.v
        a = op("Y", lambda e, rb=rb: e.dma_start(
            out=out_tm[rb * P:(rb + 1) * P, :],
            in_=out_buf[rb % 2][:, 0:D]),
            wait=[(sV, ep_done[rb])], inc=[(dOU[rb % 2], 16)])
        out_st[rb] = a["dOU%d" % (rb % 2)]

    if debug:
        dumps = [("xb", xb[:, :]), ("vq", vq[:, :]),
                 ("qw", wv_sb[:, :]), ("wt", wk_sb[:, :]),
                 ("zs", zs[:, :]), ("a_sb", a_sb[:, :]),
                 ("qs", qs[:, :]), ("ks", ks[:, :]),
                 ("scal", scal[:, 0:3]), ("sc1", sc1[0:1, 0:8]),

                 ("xfd", xf[:, :])]
        for nm, t in dumps:
            sh = [t.shape[0] if hasattr(t, 'shape') else P]
            ap = t[:, :] if not isinstance(t, bass.AP) else t
            shp = list(ap.shape)
            dt_ = ap.dtype
            dout = nc.dram_tensor("dbg_" + nm, shp, dt_,
                                  kind="ExternalOutput")
            op("Y", lambda e, dout=dout, ap=ap: e.dma_start(
                out=dout[:, :], in_=ap),
                wait=[(sV, sV.v), (sS, sS.v), (sT, sT.v)],
                inc=[(dOU[0], 16)])
    B.wait("Y", dOU[0], dOU[0].v)
    B.wait("Y", dOU[1], dOU[1].v)
    B.wait("G", dZ, dZ.v)
    B.wait("G", dMX, dMX.v)

    with nc.Block() as block:
        @block.sync
        def _(e):
            for f in B.streams["Y"]:
                f(e)

        @block.gpsimd
        def _(e):
            for f in B.streams["G"]:
                f(e)

        @block.tensor
        def _(e):
            for f in B.streams["T"]:
                f(e)

        @block.vector
        def _(e):
            for f in B.streams["V"]:
                f(e)

        @block.scalar
        def _(e):
            for f in B.streams["S"]:
                f(e)

    stack.close()
    return nc


def _prep_inputs(x, Wq, Wk, Wv, Wo, Wf, bias, n_cores, R):
    xflat = np.ascontiguousarray(np.asarray(x, np.float32).reshape(-1, D))
    wq = np.ascontiguousarray(np.asarray(Wq, np.float32).T)
    wk = np.ascontiguousarray(np.asarray(Wk, np.float32).T)
    wv = np.ascontiguousarray(np.asarray(Wv, np.float32).T)
    wo = np.ascontiguousarray(np.asarray(Wo, np.float32).T)
    wf = np.ascontiguousarray(np.asarray(Wf, np.float32).T)
    br = np.ascontiguousarray(np.asarray(bias, np.float32).reshape(1, D))
    eye = np.eye(P, dtype=np.float32)
    maps = []
    for c in range(n_cores):
        rows = xflat[c * R:(c + 1) * R]
        maps.append({
            "xT": np.ascontiguousarray(rows.T),
            "x_tm": np.ascontiguousarray(rows),
            "wqT": wq, "wkT": wk, "wvT": wv, "woT": wo, "wfT": wf,
            "bias_row": br, "ident_in": eye,
        })
    return maps


def kernel(L0_attn_x, L0_attn_Wq, L0_attn_Wk, L0_attn_Wv, L0_attn_Wo,
           L0_ffn_weight, L0_ffn_bias):
    n_cores = 8
    R = 2048
    x = np.asarray(L0_attn_x, np.float32)
    maps = _prep_inputs(x, L0_attn_Wq, L0_attn_Wk, L0_attn_Wv, L0_attn_Wo,
                        L0_ffn_weight, L0_ffn_bias, n_cores, R)
    z_groups = [[2 * b, 2 * b + 1] for b in range(n_cores // 2)]
    all_group = [list(range(n_cores))]
    nc = build_nc(n_cores, R, z_groups, all_group)
    res = run_bass_kernel_spmd(nc, maps, core_ids=list(range(n_cores)))
    out = np.empty((n_cores * R, D), np.float32)
    for c in range(n_cores):
        out[c * R:(c + 1) * R] = np.asarray(res.results[c]["out_tm"])
    return out.reshape(x.shape)


# version 21
# speedup vs baseline: 1.1084x; 1.1084x over previous
import numpy as np
import contextlib
import concourse.bass as bass
import concourse.mybir as mybir
from concourse.bass_utils import run_bass_kernel_spmd


F32 = mybir.dt.float32
BF16 = mybir.dt.bfloat16
AF = mybir.ActivationFunctionType
ALU = mybir.AluOpType
AX = mybir.AxisListType

P = 128
D = 1024
C = D // P
H = 16
MAGIC = float(np.float32(12582912.0))
EPS = 1e-5


class _Sem:
    def __init__(self, name):
        self.name = name
        self.h = None
        self.v = 0


class _Builder:
    ENGINES = ("T", "V", "S", "G", "Y")

    def __init__(self):
        self.streams = {e: [] for e in self.ENGINES}
        self.lastwait = {e: {} for e in self.ENGINES}

    def op(self, eng, fn, wait=(), inc=()):
        w = []
        for sem, val in wait:
            if val > self.lastwait[eng].get(sem.name, -1):
                w.append((sem, val))
                self.lastwait[eng][sem.name] = val
        for sem, amt in inc:
            sem.v += amt

        def run(e, fn=fn, w=tuple(w), inc=tuple(inc)):
            for sem, val in w:
                e.wait_ge(sem.h, val)
            ins = fn(e)
            for sem, amt in inc:
                ins.then_inc(sem.h, amt)
        self.streams[eng].append(run)
        return {sem.name: sem.v for sem, _ in inc}

    def wait(self, eng, sem, val):
        if val > self.lastwait[eng].get(sem.name, -1):
            self.lastwait[eng][sem.name] = val

            def run(e, sem=sem, val=val):
                e.wait_ge(sem.h, val)
            self.streams[eng].append(run)


def build_nc(n_cores, R, z_groups, all_group, debug=False):
    RB = R // P
    RGS = 512
    RG = R // RGS
    NQ = RG

    nc = bass.Bass("TRN2", target_bir_lowering=False)

    xT = nc.dram_tensor("xT", [D, R], F32, kind="ExternalInput")
    x_tm = nc.dram_tensor("x_tm", [R, D], F32, kind="ExternalInput")
    wqT = nc.dram_tensor("wqT", [D, D], F32, kind="ExternalInput")
    wkT = nc.dram_tensor("wkT", [D, D], F32, kind="ExternalInput")
    wvT = nc.dram_tensor("wvT", [D, D], F32, kind="ExternalInput")
    woT = nc.dram_tensor("woT", [D, D], F32, kind="ExternalInput")
    wfT = nc.dram_tensor("wfT", [D, D], F32, kind="ExternalInput")
    bias_row = nc.dram_tensor("bias_row", [1, D], F32, kind="ExternalInput")
    out_tm = nc.dram_tensor("out_tm", [R, D], F32, kind="ExternalOutput")

    z_in = nc.dram_tensor("z_in", [P, 512], F32, kind="Internal")
    z_out = nc.dram_tensor("z_out", [P, 512], F32, kind="Internal")
    mx_in = nc.dram_tensor("mx_in", [1, 8], F32, kind="Internal")
    mx_out = nc.dram_tensor("mx_out", [1, 8], F32, kind="Internal")
    bcst_d = nc.dram_tensor("bcst_d", [8, 512], BF16, kind="Internal")
    pcol_d = nc.dram_tensor("pcol_d", [2, P], F32, kind="Internal")
    sc_d = nc.dram_tensor("sc_d", [1, 8], F32, kind="Internal")
    ident_in = nc.dram_tensor("ident_in", [P, P], F32, kind="ExternalInput")

    stack = contextlib.ExitStack()
    sb = lambda name, shape, dt: stack.enter_context(
        nc.sbuf_tensor(name, shape, dt))
    pst = lambda name, shape, dt: stack.enter_context(
        nc.psum_tensor(name, shape, dt))

    xf = sb("xf", [P, C * R], F32)
    xb = sb("xb", [P, C * R], BF16)
    vq = sb("vq", [P, C * R], BF16)
    wv_sb = sb("wv_sb", [P, C * D], BF16)
    wq_sb = sb("wq_sb", [P, C * D], BF16)
    wk_sb = sb("wk_sb", [P, C * D], BF16)
    qw_sb = wv_sb
    qs = sb("qs", [P, D], BF16)
    ks = sb("ks", [P, D], BF16)
    zs = sb("zs", [P, 512], F32)
    a_sb = sb("a_sb", [P, 512], BF16)
    aT_sb = sb("aT_sb", [P, 256], BF16)
    bdp = sb("bdp", [P, P], BF16)
    identb = sb("identb", [P, P], BF16)
    stq = sb("stq", [P, 4], F32)
    sqs = sb("sqs", [P, 2 * 512], BF16)
    srow = sb("srow", [1, 1536], F32)
    onesb = sb("onesb", [P, 1], BF16)
    bias_sb = sb("bias_sb", [1, D], F32)
    biasa_bf = sb("biasa_bf", [1, D], BF16)
    mr_bb = sb("mr_bb", [P, 2 * RGS], BF16)
    m_bb = mr_bb[:, 0:RGS]
    r_bb = mr_bb[:, RGS:2 * RGS]
    cm_bb = sb("cm_bb", [P, RGS], BF16)
    cr_bb = sb("cr_bb", [P, RGS], BF16)
    scal = sb("scal", [P, 8], F32)
    sc1 = sb("sc1", [1, 16], F32)
    shf = sb("shf", [P, 2], F32)
    prow = sb("prow", [1, 2 * P], F32)
    ones_r = sb("ones_r", [1, P], BF16)
    red_w = sb("red_w", [P, 16], F32)
    cst = sb("cst", [P, 8], F32)
    wfs = sb("wfs", [P, D], F32)
    if R >= 2 * D:
        xtm_buf = [xf[:, (2 * i) * R + R // 2: (2 * i) * R + R // 2 + D]
                   for i in (0, 1)]
        out_buf = [xf[:, (2 * i + 1) * R + R // 2:
                      (2 * i + 1) * R + R // 2 + D] for i in (0, 1)]
    else:
        xtm_t = sb("xtm_t", [P, 2 * D], F32)
        outb_t = sb("outb_t", [P, 2 * D], F32)
        xtm_buf = [xtm_t[:, 0:D], xtm_t[:, D:2 * D]]
        out_buf = [outb_t[:, 0:D], outb_t[:, D:2 * D]]

    pb = [pst("pb%d" % i, [P, 512], F32) for i in range(8)]

    dY = _Sem("dY"); dS = _Sem("dS")
    dXL = [_Sem("dXL%d" % i) for i in range(8)]
    dQR = _Sem("dQR")
    dWV = _Sem("dWV"); dWQ = _Sem("dWQ"); dWK = _Sem("dWK")
    dWF = [_Sem("dWFa"), _Sem("dWFb")]
    dWO = _Sem("dWO"); dBI = _Sem("dBI")
    dZ = _Sem("dZ"); dMX = _Sem("dMX")
    dXT = [_Sem("dXT0"), _Sem("dXT1")]
    dOU = [_Sem("dOU0"), _Sem("dOU1")]
    sT = _Sem("sT"); sV = _Sem("sV"); sS = _Sem("sS")
    sG = _Sem("sG")
    cC = _Sem("cC")
    dBC = _Sem("dBC"); dPC = _Sem("dPC"); dSC = _Sem("dSC")
    dCR = _Sem("dCR")
    dID = _Sem("dID")
    for s in (dY, dS, dXL[0], dXL[1], dXL[2], dXL[3],
              dXL[4], dXL[5], dXL[6], dXL[7], dQR,
              dWV, dWQ, dWK, dWF[0], dWF[1], dWO, dBI,
              dZ, dMX, dXT[0], dXT[1], dOU[0], dOU[1],
              sT, sV, sS, sG, cC, dBC, dPC, dSC, dCR, dID):
        s.h = stack.enter_context(nc.semaphore(s.name))

    nc.const_aps.register(F32, MAGIC, cst[:, 1:2])
    nc.const_aps.register(F32, -1.0, cst[:, 4:5])
    nc.const_aps.register(F32, -MAGIC, cst[:, 5:6])

    B = _Builder()
    op = B.op

    def xf_c(c):
        return xf[:, c * R:(c + 1) * R]

    def xb_c(c):
        return xb[:, c * R:(c + 1) * R]

    def vq_c(c):
        return vq[:, c * R:(c + 1) * R]

    def qin_c(c):
        return xf[:, c * R:c * R + R // 2].bitcast(BF16)

    def w_ck(w, dc, lo, n):
        return w[:, dc * D + lo: dc * D + lo + n]

    def w_load(dst, src):
        return lambda e: e.dma_start(
            out=dst[:, :],
            in_=bass.AP(src, 0, [[D, P], [P * D, C], [1, D]]))

    _xq = {0: "Y", 3: "Y", 6: "Y", 1: "S", 4: "S", 7: "S",
           2: "G", 5: "G"}
    for c in (0, 1, 2, 3, 4, 5, 6, 7):
        op(_xq[c], lambda e, c=c: e.dma_start(
            out=xf_c(c), in_=xT[c * P:(c + 1) * P, :]),
            inc=[(dXL[c], 16)])
    xld = {c: (dXL[c], 16) for c in range(C)}

    for j, val in enumerate((EPS, MAGIC, MAGIC + 1.0, 2.0, -1.0, -MAGIC)):
        op("V", lambda e, j=j, val=val: e.memset(cst[:, j:j + 1], val),
           inc=([(sV, 1)] if j == 5 else []))

    op("G", lambda e: e.dma_start(out=identb[:, :], in_=ident_in[:, :]),
       inc=[(dID, 16)])
    g_ident = 16
    op("V", lambda e: e.memset(onesb[:, :], 1.0))
    a = op("V", lambda e: e.memset(ones_r[:, :], 1.0), inc=[(sV, 1)])
    v_ones = sV.v

    op("G", lambda e: e.dma_start(out=bias_sb[:, :], in_=bias_row[:, :]),
       inc=[(dBI, 16)])
    g_bias = (dBI, 16)
    op("G", w_load(wv_sb, wvT), inc=[(dWV, 16)])
    g_wv = (dWV, 16)

    cast_done = {}
    for c in range(C):
        a = op("V", lambda e, c=c: e.tensor_copy(xb_c(c), xf_c(c)),
               wait=[xld[c]], inc=[(sV, 1)])
        cast_done[c] = sV.v
    v_cast_all = sV.v

    corr_done = {}
    corr_sub = {}
    vp_rg_evac = {}
    vp_rg_tdone = {}
    g_wq = g_wk = None
    wf_ld = {}
    wf1_red = {}

    def wfh(u):
        return wfs[:, (u % 2) * 512:(u % 2) * 512 + 512]

    def wf_src(u):
        c, half = u // 2, u % 2
        return wfT[c * P:(c + 1) * P, half * 512: half * 512 + 512]

    def emit_wf1_pair2(k):
        for u in (2 * k, 2 * k + 1):
            w = [(sV, wf1_red[u - 2])] if u >= 2 else []
            a = op("Y", lambda e, u=u: e.dma_start(
                out=wfh(u), in_=wf_src(u)),
                wait=w, inc=[(dWF[u % 2], 16)])
            wf_ld[u] = a["dWF%s" % ("ab"[u % 2])]
        for u in (2 * k, 2 * k + 1):
            op("V", lambda e, u=u: e.tensor_reduce(
                red_w[:, u:u + 1], wfh(u),
                AX.X, ALU.add, apply_absolute_value=True),
                wait=[(dWF[u % 2], wf_ld[u])], inc=[(sV, 1)])
            wf1_red[u] = sV.v
    quad_gbco = {}
    sq_done = {}
    sq_used = {}
    stat_done = {}
    srow_rd = {}

    def emit_stats_mms(q):
        for c in range(C):
            ping = (q * C + c) % 2
            vw = [xld[c], (sV, cast_done[c])]
            if (q * C + c) >= 2:
                vw.append((sT, sq_used[(q * C + c) - 2]))
            op("S", lambda e, c=c, q=q, ping=ping: e.activation(
                sqs[:, ping * 512:(ping + 1) * 512],
                xb_c(c)[:, q * RGS: q * RGS + RGS], AF.Square),
                wait=vw, inc=[(sS, 1)])
            sq_done[q * C + c] = sS.v
            tw = [(sS, sq_done[q * C + c]), (sV, v_ones)]
            if c == 0 and q >= 1:
                tw.append((sV, srow_rd[q - 1]))
            op("T", lambda e, c=c, q=q: e.matmul(
                pb[6][0:1, 0:512], onesb[:, :],
                xb_c(c)[:, q * RGS: q * RGS + RGS],
                start=(c == 0), stop=(c == C - 1)),
                wait=tw)
            op("T", lambda e, c=c, ping=ping: e.matmul(
                pb[7][0:1, 0:512], onesb[:, :],
                sqs[:, ping * 512:(ping + 1) * 512],
                start=(c == 0), stop=(c == C - 1)),
                inc=[(sT, 1)])
            sq_used[q * C + c] = sT.v
        stat_done[q] = sT.v

    def emit_stats_post(q):
        t_stat = stat_done[q]
        op("V", lambda e: e.tensor_scalar(
            srow[0:1, 0:512], pb[6][0:1, 0:512], 1.0 / D, None, ALU.mult),
            wait=[(sT, t_stat)], inc=[(sV, 1)])
        op("V", lambda e: e.tensor_scalar(
            srow[0:1, 512:1024], pb[7][0:1, 0:512], 1.0 / D, None,
            ALU.mult), inc=[(sV, 1)])
        srow_rd[q] = sV.v
        B.wait("V", sV, sV.v)
        op("V", lambda e: e.tensor_tensor(
            srow[0:1, 1024:1536], srow[0:1, 0:512], srow[0:1, 0:512],
            ALU.mult), inc=[(sV, 1)])
        B.wait("V", sV, sV.v)
        op("V", lambda e: e.tensor_tensor(
            srow[0:1, 512:1024], srow[0:1, 512:1024], srow[0:1, 1024:1536],
            ALU.subtract), inc=[(sV, 1)])
        v_var = sV.v
        op("S", lambda e: e.activation(
            srow[0:1, 1024:1536], srow[0:1, 512:1024], AF.Ln,
            bias=cst[0:1, 0:1]), wait=[(sV, v_var)], inc=[(sS, 1)])
        B.wait("S", sS, sS.v)
        a = op("S", lambda e: e.activation(
            srow[0:1, 1024:1536], srow[0:1, 1024:1536], AF.Exp,
            scale=-0.5), inc=[(sS, 1)])
        s_rstd = sS.v
        gw = [(sS, s_rstd)]
        if q >= 1:
            gw.append((sV, corr_sub[q - 1]))
        a = op("G", lambda e, q=q: e.dma_start(
            out=bcst_d[q:q + 1, :], in_=srow[0:1, 0:512]),
            wait=gw, inc=[(dBC, 16)])
        a = op("G", lambda e, q=q: e.dma_start(
            out=bcst_d[4 + q:5 + q, :], in_=srow[0:1, 1024:1536]),
            inc=[(dBC, 16)])
        quad_gbco[q] = a["dBC"]
        a = op("G", lambda e, q=q: e.dma_start(
            out=cm_bb[:, :],
            in_=bass.AP(bcst_d, q * 512, [[0, P], [1, 512]])),
            wait=[(dBC, quad_gbco[q])], inc=[(dBC, 16)])
        g_corr = a["dBC"]
        for c in range(C):
            op("V", lambda e, c=c, q=q: e.tensor_tensor(
                xb_c(c)[:, q * RGS: q * RGS + RGS],
                xb_c(c)[:, q * RGS: q * RGS + RGS], cm_bb[:, :],
                ALU.subtract),
                wait=[(dBC, g_corr), (sV, cast_done[c])], inc=[(sV, 1)])
        corr_sub[q] = sV.v

    emit_stats_mms(0)
    emit_stats_post(0)
    if NQ > 1:
        emit_stats_mms(1)
        emit_stats_post(1)
        emit_wf1_pair2(0)
    op("G", w_load(wq_sb, wqT), inc=[(dWQ, 16)])
    g_wq = (dWQ, 16)
    op("G", w_load(wk_sb, wkT), inc=[(dWK, 16)])
    g_wk = (dWK, 16)
    vp_evac = {}
    vp_bank_last = {}
    for rg in range(RG):
        q = rg

        for ocb in range(C):
            bank = pb[ocb % 6]
            w0 = [(sV, corr_sub[q]), g_wv]
            if (ocb % 6) in vp_bank_last:
                w0.append((sS, vp_bank_last[ocb % 6]))
            for dc in range(C):
                op("T", lambda e, rg=rg, ocb=ocb, dc=dc, bank=bank: e.matmul(
                    bank[:, 0:512],
                    wv_sb[:, dc * D + ocb * P: dc * D + (ocb + 1) * P],
                    xb_c(dc)[:, rg * RGS: rg * RGS + RGS],
                    start=(dc == 0), stop=(dc == C - 1)),
                    wait=(w0 if dc == 0 else []),
                    inc=([(sT, 1)] if dc == C - 1 else []))
            t_g = sT.v
            sw = [(sT, t_g)]

            a = op("S", lambda e, rg=rg, ocb=ocb, bank=bank: e.copy(
                vq_c(ocb)[:, rg * RGS: rg * RGS + RGS], bank[:, 0:512]),
                wait=sw, inc=[(sS, 1)])
            vp_evac[(rg, ocb)] = sS.v
            vp_bank_last[ocb % 6] = sS.v
        vp_rg_evac[rg] = sS.v
        vp_rg_tdone[rg] = sT.v
        if rg + 2 < NQ:
            emit_stats_mms(rg + 2)
            emit_stats_post(rg + 2)
            emit_wf1_pair2(rg + 1)
    vp_last = sS.v
    t_vp_last = sT.v

    for q in range(NQ):
        rw = [(dBC, quad_gbco[q])]
        if q >= 1:
            rw.append((sV, corr_done[q - 1][1]))
        op("G", lambda e, q=q: e.dma_start(
            out=cr_bb[:, :],
            in_=bass.AP(bcst_d, (4 + q) * 512, [[0, P], [1, 512]])),
            wait=rw, inc=[(dCR, 16)])
        for c in range(C):
            cw = [] if c else [(dCR, 16 * (q + 1)), (sV, corr_sub[q]),
                               (sT, vp_rg_tdone[q])]
            op("V", lambda e, c=c, q=q: e.tensor_tensor(
                xb_c(c)[:, q * RGS: q * RGS + RGS],
                xb_c(c)[:, q * RGS: q * RGS + RGS], cr_bb[:, :],
                ALU.mult), wait=cw, inc=[(sV, 1)])
        corr_done[q] = (sV, sV.v)

    nrm_state = {"done": {}, "amax_parts": 0, "vqr": {}}

    def emit_norm_quad(q, part):
        if part == 0:
            base = 3 * NQ * 16 + 32 * q
            a = op("G", lambda e, q=q: e.dma_start(
                out=m_bb[:, :],
                in_=bass.AP(bcst_d, q * 512, [[0, P], [1, 512]])),
                wait=[(dBC, base),
                      (sV, nrm_state["done"].get(q - 1, 0)),
                      (sV, nrm_state["vqr"].get(q - 1, 0))],
                inc=[(dBC, 16)])
            a = op("G", lambda e, q=q: e.dma_start(
                out=r_bb[:, :],
                in_=bass.AP(bcst_d, (4 + q) * 512, [[0, P], [1, 512]])),
                inc=[(dBC, 16)])
            g_n = a["dBC"]
            for c in range(C):
                op("V", lambda e, c=c, q=q: e.tensor_tensor(
                    xf_c(c)[:, q * RGS: q * RGS + RGS],
                    xf_c(c)[:, q * RGS: q * RGS + RGS], m_bb[:, :],
                    ALU.subtract),
                    wait=[(dBC, g_n), (sV, cast_done[c])], inc=[(sV, 1)])
            B.wait("V", sV, sV.v)
        else:
            for c in range(C):
                op("V", lambda e, c=c, q=q: e.tensor_tensor(
                    xf_c(c)[:, q * RGS: q * RGS + RGS],
                    xf_c(c)[:, q * RGS: q * RGS + RGS], r_bb[:, :],
                    ALU.mult), inc=[(sV, 1)])
            nrm_state["done"][q] = sV.v
        return sV.v

    def emit_absmax_part(q, h):
        for c in range(4 * h, 4 * h + 4):
            op("V", lambda e, c=c, q=q: e.tensor_reduce(
                stq[:, 0:1], xf_c(c)[:, q * RGS: q * RGS + RGS],
                AX.X, ALU.max, apply_absolute_value=True),
                wait=[(sV, nrm_state["done"][q])], inc=[(sV, 1)])
            B.wait("V", sV, sV.v)
            if nrm_state["amax_parts"] == 0:
                op("V", lambda e: e.tensor_copy(stq[:, 1:2], stq[:, 0:1]),
                   inc=[(sV, 1)])
            else:
                op("V", lambda e: e.tensor_max(stq[:, 1:2], stq[:, 1:2],
                                               stq[:, 0:1]), inc=[(sV, 1)])
            nrm_state["amax_parts"] += 1
            B.wait("V", sV, sV.v)
        return sV.v

    mxs = {"g_mx": None, "staged": False}

    def _mx1():
        emit_mx_stage1()
        mxs["staged"] = True

    def _mx2():
        mxs["g_mx"] = emit_mx_stage2()

    def emit_vq_rstd(q):
        w = [(sS, vp_rg_evac[q])]
        for ocb in range(C):
            op("V", lambda e, ocb=ocb, q=q: e.tensor_tensor(
                vq_c(ocb)[:, q * RGS: q * RGS + RGS],
                vq_c(ocb)[:, q * RGS: q * RGS + RGS], r_bb[:, :],
                ALU.mult),
                wait=(w if ocb == 0 else []), inc=[(sV, 1)])
        nrm_state["vqr"][q] = sV.v

    slot_items = []
    for q in range(NQ):
        slot_items.append((5.5, lambda q=q: emit_norm_quad(q, 0), False))
        slot_items.append((5.5, lambda q=q: emit_norm_quad(q, 1), False))
        slot_items.append((2.6, lambda q=q: emit_vq_rstd(q), False))
        for h in range(2):
            slot_items.append(
                (3.3, lambda q=q, h=h: emit_absmax_part(q, h), False))
    slot_items.append((2.5, _mx1, True))
    slot_items.append((2.5, _mx2, True))
    slot_cursor = 0

    tail_budget = 42.0
    tail_used = 0.0

    def _tail_ok():
        return (slot_cursor < len(slot_items)
                and not slot_items[slot_cursor][2]
                and tail_used + slot_items[slot_cursor][0] <= tail_budget)
    while len(wf1_red) < 2 * C or _tail_ok():
        if len(wf1_red) < 2 * C:
            emit_wf1_pair2(len(wf1_red) // 2)
        if _tail_ok():
            tail_used += slot_items[slot_cursor][0]
            slot_items[slot_cursor][1]()
            slot_cursor += 1
    B.wait("V", sV, sV.v)
    op("V", lambda e: e.tensor_reduce(shf[:, 0:1], red_w[:, 0:16], AX.X,
                                      ALU.add), inc=[(sV, 1)])
    v_wtot = sV.v
    a = op("G", lambda e: e.dma_start(out=pcol_d[1:2, :], in_=shf[:, 0:1]),
           wait=[(sV, v_wtot)], inc=[(dPC, 16)])
    a = op("G", lambda e: e.dma_start(out=prow[0:1, P:2 * P],
                                      in_=pcol_d[1:2, :]),
           wait=[(dPC, 16)], inc=[(dPC, 16)])
    def emit_winv_stageB():
        a = op("V", lambda e: e.tensor_reduce(
            sc1[0:1, 1:2], prow[0:1, P:2 * P], AX.X, ALU.add),
            wait=[(dPC, 32)], inc=[(sV, 1)])
        B.wait("V", sV, sV.v)
        a = op("V", lambda e: e.tensor_scalar(
            sc1[0:1, 2:3], sc1[0:1, 1:2], 1.0 / (D * D), None, ALU.mult),
            inc=[(sV, 1)])
        B.wait("V", sV, sV.v)
        op("V", lambda e: e.reciprocal(sc1[0:1, 3:4], sc1[0:1, 2:3]),
           inc=[(sV, 1)])
        v_winv = sV.v
        a = op("Y", lambda e: e.dma_start(out=sc_d[0:1, 2:3],
                                          in_=sc1[0:1, 3:4]),
               wait=[(sV, v_winv)], inc=[(dSC, 16)])
        a = op("Y", lambda e: e.dma_start(
            out=scal[:, 2:3], in_=bass.AP(sc_d, 2, [[0, P], [1, 1]])),
            wait=[(dSC, 16)], inc=[(dSC, 16)])
    g_winv = 32

    def emit_mx_stage1():
        op("V", lambda e: e.tensor_copy(shf[:, 0:1], stq[:, 1:2]),
           wait=[(dPC, 32)], inc=[(sV, 1)])
        a = op("G", lambda e: e.dma_start(out=pcol_d[0:1, :],
                                          in_=shf[:, 0:1]),
               wait=[(sV, sV.v)], inc=[(dPC, 16)])
        a = op("G", lambda e: e.dma_start(out=prow[0:1, 0:P],
                                          in_=pcol_d[0:1, :]),
               wait=[(dPC, 48)], inc=[(dPC, 16)])

    def emit_mx_stage2():
        a = op("V", lambda e: e.tensor_reduce(
            shf[0:1, 1:2], prow[0:1, 0:P], AX.X, ALU.max),
            wait=[(dPC, 64)], inc=[(sV, 1)])
        B.wait("V", sV, sV.v)
        op("V", lambda e: e.memset(sc1[0:1, 8:16], 0.0), inc=[(sV, 1)])
        B.wait("V", sV, sV.v)
        a = op("V", lambda e: e.tensor_scalar(
            sc1[0:1, 8:16], sc1[0:1, 8:16], shf[0:1, 1:2], None, ALU.add),
            inc=[(sV, 1)])
        v_mx8 = sV.v
        a = op("G", lambda e: e.dma_start(
            out=mx_in[0:1, 0:8], in_=sc1[0:1, 8:16]),
            wait=[(sV, v_mx8)], inc=[(dMX, 16)])
        a = op("G", lambda e: e.collective_compute(
            "AllReduce", ALU.max, replica_groups=all_group,
            ins=[mx_in.ap().opt()], outs=[mx_out.ap().opt()]),
            wait=[(dMX, 16)], inc=[(cC, 1)])
        a = op("G", lambda e: e.dma_start(out=sc1[0:1, 0:1],
                                          in_=mx_out[0:1, 0:1]),
               wait=[(cC, cC.v)], inc=[(dMX, 16)])
        return (dMX, 32)

    wf2_ld = {}
    wf2_use = {}

    def emit_wf2_load(u):
        w = [(sV, v_wtot)] if u < 2 else [(sS, wf2_use[u - 2])]
        a = op("Y", lambda e, u=u: e.dma_start(
            out=wfh(u), in_=wf_src(u)), wait=w, inc=[(dWF[u % 2], 16)])
        wf2_ld[u] = a["dWF%s" % ("ab"[u % 2])]

    def emit_qw_slot(c, half):
        u = 2 * c + half
        if u not in wf2_ld:
            emit_wf2_load(u)
        if u + 1 not in wf2_ld and u + 1 < 2 * C:
            emit_wf2_load(u + 1)
        w = [(dSC, g_winv), (sS, sS.v), (dWF[u % 2], wf2_ld[u]),
             (sT, t_vp_last)]
        op("S", lambda e, u=u: e.activation(
            pb[6][:, 0:512],
            wfh(u),
            AF.Copy, bias=MAGIC, scale=scal[:, 2:3]),
            wait=w + [(sV, v_wtot)], inc=[(sS, 1)])
        wf2_use[u] = sS.v
        B.wait("S", sS, sS.v)
        op("S", lambda e: e.activation(pb[7][:, 0:512], pb[6][:, 0:512],
                                       AF.Relu, bias=cst[:, 2:3],
                                       scale=-1.0), inc=[(sS, 1)])
        B.wait("S", sS, sS.v)
        op("S", lambda e: e.activation(pb[6][:, 0:512], pb[7][:, 0:512],
                                       AF.Relu, bias=cst[:, 3:4], scale=-1.0),
           inc=[(sS, 1)])
        B.wait("S", sS, sS.v)
        op("S", lambda e, c=c, half=half: e.activation(
            qw_sb[:, c * D + half * 512: c * D + half * 512 + 512],
            pb[6][:, 0:512], AF.Copy, bias=-1.0), inc=[(sS, 1)])
        return sS.v

    def emit_quant_chunk(c, g_scal_val, eng):
        nd = nrm_state["done"][NQ - 1]
        if eng == "S":
            op("S", lambda e, c=c: e.activation(
                xf_c(c), xf_c(c), AF.Copy, bias=MAGIC, scale=scal[:, 0:1]),
                wait=[(sV, nd), (dSC, g_scal_val)], inc=[(sS, 1)])
            s1 = (sS, sS.v)
        else:
            op("V", lambda e, c=c: e.tensor_scalar(
                xf_c(c), xf_c(c), scal[:, 0:1], MAGIC, ALU.mult, ALU.add),
                wait=[(sV, nd), (dSC, g_scal_val)], inc=[(sV, 1)])
            s1 = (sV, sV.v)
        op("S", lambda e, c=c: e.activation(
            qin_c(c), xf_c(c), AF.Copy, bias=-MAGIC),
            wait=[s1], inc=[(sS, 1)])
        return ("S", (sS, sS.v))

    def emit_sc1_chain(g_mx_w):
        a = op("V", lambda e: e.tensor_scalar(
            sc1[0:1, 4:5], sc1[0:1, 0:1], 1.0 / 127.0, None, ALU.mult),
            wait=[g_mx_w], inc=[(sV, 1)])
        B.wait("V", sV, sV.v)
        op("V", lambda e: e.reciprocal(sc1[0:1, 5:6], sc1[0:1, 4:5]),
           inc=[(sV, 1)])
        B.wait("V", sV, sV.v)
        op("V", lambda e: e.tensor_tensor(
            sc1[0:1, 6:7], sc1[0:1, 2:3], sc1[0:1, 4:5], ALU.mult),
            inc=[(sV, 1)])
        v_alpha = sV.v
        B.wait("V", sV, sV.v)
        op("V", lambda e: e.reciprocal(sc1[0:1, 7:8], sc1[0:1, 6:7]),
           wait=[g_bias], inc=[(sV, 1)])
        B.wait("V", sV, sV.v)
        op("V", lambda e: e.tensor_scalar(
            biasa_bf[0:1, :], bias_sb[0:1, :], sc1[0:1, 7:8], None,
            ALU.mult), inc=[(sV, 1)])
        vb = sV.v
        a = op("Y", lambda e: e.dma_start(
            out=sc_d[0:1, 0:1], in_=sc1[0:1, 5:6]),
            wait=[(sV, v_alpha), (dSC, g_winv)], inc=[(dSC, 16)])
        a = op("Y", lambda e: e.dma_start(
            out=sc_d[0:1, 1:2], in_=sc1[0:1, 6:7]),
            inc=[(dSC, 16)])
        a = op("Y", lambda e: e.dma_start(
            out=sc_d[0:1, 3:4], in_=sc1[0:1, 7:8]),
            wait=[(sV, vb)], inc=[(dSC, 16)])
        a = op("Y", lambda e: e.dma_start(
            out=scal[:, 0:1], in_=bass.AP(sc_d, 0, [[0, P], [1, 1]])),
            wait=[(dSC, 80)], inc=[(dSC, 16)])
        a = op("Y", lambda e: e.dma_start(
            out=scal[:, 1:2], in_=bass.AP(sc_d, 1, [[0, P], [1, 1]])),
            inc=[(dSC, 16)])
        a = op("Y", lambda e: e.dma_start(
            out=scal[:, 3:4], in_=bass.AP(sc_d, 3, [[0, P], [1, 1]])),
            inc=[(dSC, 16)])
        return vb, 128

    qw_slots = [(c, half) for c in range(C) for half in range(2)]
    qw_cursor = 0
    quant_cursor = 0
    s_qw = None
    quant_done = {}

    qk_done = {}
    qevac = {}
    kevac = {}
    zfirst = {4: True, 5: True}
    for rb in range(RB):
        q = rb // 4
        w0 = [corr_done[q], g_wq, g_wk]
        if rb == 0:
            w0.append((sS, vp_last))
        else:
            w0.append((sS, qevac[rb - 1]))
            w0.append(kevac[rb - 1])
        first = True
        for dc in range(C):
            for half in range(2):
                op("T", lambda e, rb=rb, dc=dc, half=half: e.matmul(
                    pb[half][:, 0:512],
                    xb_c(dc)[:, rb * P:(rb + 1) * P],
                    w_ck(wq_sb, dc, half * 512, 512),
                    start=(dc == 0), stop=(dc == C - 1)),
                    wait=(w0 if first else []))
                first = False
                op("T", lambda e, rb=rb, dc=dc, half=half: e.matmul(
                    pb[2 + half][:, 0:512],
                    xb_c(dc)[:, rb * P:(rb + 1) * P],
                    w_ck(wk_sb, dc, half * 512, 512),
                    start=(dc == 0), stop=(dc == C - 1)),
                    inc=([(sT, 1)] if dc == C - 1 and half == 1 else []))
        qk_done[rb] = sT.v
        sw = [(sT, qk_done[rb])]
        op("S", lambda e, rb=rb: e.copy(
            qs[:, 0:512], pb[0][:, 0:512]), wait=sw)
        op("S", lambda e, rb=rb: e.copy(
            ks[:, 0:512], pb[2][:, 0:512]), inc=[(sS, 1)])
        qevac[rb] = sS.v
        if rb < 6:
            op("S", lambda e, rb=rb: e.copy(
                qs[:, 512:1024], pb[1][:, 0:512]))
            op("S", lambda e, rb=rb: e.copy(
                ks[:, 512:1024], pb[3][:, 0:512]), inc=[(sS, 1)])
            qevac[rb] = sS.v
            kevac[rb] = (sS, sS.v)
        else:
            op("V", lambda e, rb=rb: e.tensor_copy(
                qs[:, 512:1024], pb[1][:, 0:512]), wait=sw)
            op("V", lambda e, rb=rb: e.tensor_copy(
                ks[:, 512:1024], pb[3][:, 0:512]), inc=[(sV, 1)])
            kevac[rb] = (sV, sV.v)
        for g in range(8):
            bank = pb[4 + g // 4]
            lo = (g % 4) * P
            st = zfirst[4 + g // 4]
            zfirst[4 + g // 4] = False
            zw = []
            if g == 0:
                zw = [(sS, qevac[rb]), kevac[rb]]
            op("T", lambda e, g=g, bank=bank, lo=lo, st=st, rb=rb: e.matmul(
                bank[:, lo:lo + P],
                ks[:, g * P:(g + 1) * P], qs[:, g * P:(g + 1) * P],
                start=st, stop=(rb == RB - 1 and (g % 4) == 3)),
                wait=zw, inc=([(sT, 1)] if g == 7 else []))
        used = 0.0
        if rb == 0:
            emit_winv_stageB()
            used += 2.0
        while slot_cursor < len(slot_items) and (
                used == 0.0
                or used + slot_items[slot_cursor][0] <= 8.0):
            used += slot_items[slot_cursor][0]
            slot_items[slot_cursor][1]()
            slot_cursor += 1
        if 2 <= rb < 10 and qw_cursor < len(qw_slots):
            for _ in range(2):
                if qw_cursor < len(qw_slots):
                    cc_, hh_ = qw_slots[qw_cursor]
                    s_qw = emit_qw_slot(cc_, hh_)
                    qw_cursor += 1
    t_zdone = sT.v
    while slot_cursor < len(slot_items):
        slot_items[slot_cursor][1]()
        slot_cursor += 1
    while qw_cursor < len(qw_slots):
        cc_, hh_ = qw_slots[qw_cursor]
        s_qw = emit_qw_slot(cc_, hh_)
        qw_cursor += 1

    op("G", w_load(wk_sb, woT), wait=[(sT, qk_done[RB - 1])],
       inc=[(dWO, 16)])
    g_wo = (dWO, 16)

    v_biasa, g_scal = emit_sc1_chain(mxs["g_mx"])

    for g in range(8):
        bank = pb[4 + g // 4]
        lo = (g % 4) * P
        op("V", lambda e, g=g, bank=bank, lo=lo: e.tensor_copy(
            zs[0:64, g * 64:(g + 1) * 64], bank[0:64, lo:lo + 64]),
            wait=([(sT, t_zdone)] if g == 0 else []), inc=[(sV, 1)])
        op("V", lambda e, g=g, bank=bank, lo=lo: e.tensor_copy(
            zs[64:128, g * 64:(g + 1) * 64],
            bank[64:128, lo + 64:lo + 128]), inc=[(sV, 1)])
    v_zevac = sV.v
    a = op("G", lambda e: e.dma_start(out=z_in[:, :], in_=zs[:, :]),
           wait=[(sV, v_zevac)], inc=[(dZ, 16)])
    a = op("G", lambda e: e.collective_compute(
        "AllReduce", ALU.add, replica_groups=z_groups,
        ins=[z_in.ap().opt()], outs=[z_out.ap().opt()]),
        wait=[(dZ, 16)], inc=[(cC, 1)])
    c_z = cC.v
    while quant_cursor < C:
        _, quant_done[quant_cursor] = emit_quant_chunk(
            quant_cursor, g_scal, "V" if quant_cursor % 2 == 0 else "S")
        quant_cursor += 1
    s_quant = sS.v
    v_quant = sV.v
    g_quant = sG.v
    a = op("G", lambda e: e.dma_start(out=zs[:, :], in_=z_out[:, :]),
           wait=[(cC, c_z)], inc=[(dZ, 16)])
    g_zr = (dZ, 32)

    for g in range(8):
        op("V", lambda e, g=g: e.tensor_reduce(
            red_w[:, g:g + 1], zs[:, g * 64:(g + 1) * 64], AX.X, ALU.max),
            wait=([g_zr] if g == 0 else []), inc=[(sV, 1)])
    B.wait("V", sV, sV.v)
    for g in range(8):
        op("V", lambda e, g=g: e.tensor_scalar(
            red_w[:, g:g + 1], red_w[:, g:g + 1], -0.125, None, ALU.mult),
            inc=[(sV, 1)])
    v_zmax = sV.v
    for g in range(8):
        op("S", lambda e, g=g: e.activation(
            zs[:, g * 64:(g + 1) * 64], zs[:, g * 64:(g + 1) * 64],
            AF.Exp, bias=red_w[:, g:g + 1], scale=0.125),
            wait=([(sV, v_zmax)] if g == 0 else []), inc=[(sS, 1)])
    s_exp = sS.v
    for g in range(8):
        op("V", lambda e, g=g: e.tensor_reduce(
            red_w[:, g:g + 1], zs[:, g * 64:(g + 1) * 64], AX.X, ALU.add),
            wait=([(sS, s_exp)] if g == 0 else []), inc=[(sV, 1)])
    B.wait("V", sV, sV.v)
    op("V", lambda e: e.reciprocal(red_w[:, 0:8], red_w[:, 0:8]),
       inc=[(sV, 1)])
    B.wait("V", sV, sV.v)
    for g in range(8):
        op("V", lambda e, g=g: e.tensor_scalar(
            a_sb[:, g * 64:(g + 1) * 64], zs[:, g * 64:(g + 1) * 64],
            red_w[:, g:g + 1], None, ALU.mult), inc=[(sV, 1)])
    v_soft = sV.v

    ep_done = {}
    xtm2_ld = {}
    out_st = {}

    def emit_ffn(rb, extra=()):
        fb = [pb[4 + 2 * (rb % 2)], pb[4 + 2 * (rb % 2) + 1]]
        wt = [(sS, s_qw), (sV, v_biasa),
              (sV, v_ones), (dSC, g_scal)] + list(extra)
        if rb >= 2:
            wt.append((sV, ep_done[rb - 2]))
        for half in range(2):
            op("T", lambda e, half=half, fb=fb: e.matmul(
                fb[half][:, 0:512], ones_r[0:1, :],
                biasa_bf[0:1, half * 512: half * 512 + 512],
                start=True, stop=False), wait=(wt if half == 0 else []))
        for dc in range(C):
            for half in range(2):
                op("T", lambda e, rb=rb, dc=dc, half=half, fb=fb: e.matmul(
                    fb[half][:, 0:512],
                    qin_c(dc)[:, rb * P:(rb + 1) * P],
                    w_ck(qw_sb, dc, half * 512, 512),
                    start=False, stop=False),
                    wait=([quant_done[dc]] if half == 0 else []))
        return fb


    ffn_pre = {0: emit_ffn(0, extra=((sV, v_zevac),)),
               1: emit_ffn(1, extra=((sV, v_zevac),))}

    wtil_evac = {}
    bd_built = {}
    tr_done = {}
    abd_done = {}
    wtilmm_done = {}
    for mc in range(C):
        g = mc
        vw = [(sV, v_soft)]
        if mc >= 1:
            vw.append((sT, tr_done[mc - 1]))
        op("V", lambda e: e.memset(bdp[:, :], 0.0), wait=vw, inc=[(sV, 1)])
        B.wait("V", sV, sV.v)
        op("V", lambda e, g=g: e.tensor_copy(
            bdp[0:64, 0:64], a_sb[0:64, g * 64:(g + 1) * 64]), inc=[(sV, 1)])
        op("V", lambda e, g=g: e.tensor_copy(
            bdp[64:128, 64:128], a_sb[64:128, g * 64:(g + 1) * 64]),
            inc=[(sV, 1)])
        bd_built[mc] = sV.v
        tb = pb[mc % 2]
        tw = [(sV, bd_built[mc]), (dID, g_ident)]
        if mc == 0:
            tw.append((sS, qevac[RB - 1]))
        if mc >= 2:
            tw.append((sS, abd_done[mc - 2]))
        a = op("T", lambda e, tb=tb: e.transpose(
            tb[:, 0:64].bitcast(BF16), bdp[:, :], identb[:, :]),
            wait=tw, inc=[(sT, 1)])
        tr_done[mc] = sT.v
        a = op("S", lambda e, mc=mc, tb=tb: e.copy(
            aT_sb[:, (mc % 2) * P:(mc % 2) * P + P],
            tb[:, 0:64].bitcast(BF16)),
            wait=[(sT, tr_done[mc])] +
                 ([(sT, wtilmm_done[mc - 1])] if mc >= 1 else []),
            inc=[(sS, 1)])
        abd_done[mc] = sS.v
        s_abd = sS.v
        for half in range(2):
            wt = [(sS, s_abd), g_wo]
            if mc >= 1:
                wt.append((sS, wtil_evac[mc - 1]))
            op("T", lambda e, mc=mc, half=half: e.matmul(
                pb[2 + half][:, 0:512],
                aT_sb[:, (mc % 2) * P:(mc % 2) * P + P],
                wk_sb[:, mc * D + half * 512: mc * D + half * 512 + 512],
                start=True, stop=True), wait=wt,
                inc=([(sT, 1)] if half == 1 else []))
        wtilmm_done[mc] = sT.v
        for half in range(2):
            op("S", lambda e, mc=mc, half=half: e.activation(
                wk_sb[:, mc * D + half * 512: mc * D + half * 512 + 512],
                pb[2 + half][:, 0:512], AF.Copy, scale=scal[:, 3:4]),
                wait=[(sT, wtilmm_done[mc]), (dSC, g_scal)],
                inc=([(sS, 1)] if half == 1 else []))
        wtil_evac[mc] = sS.v
    s_wtil = sS.v


    for rb in range(RB):
        w = [(sS, s_quant), (sV, v_quant), (sG, g_quant)]
        if rb >= 2:
            w.append((sV, ep_done[rb - 2]))
        a = op("Y", lambda e, rb=rb: e.dma_start(
            out=xtm_buf[rb % 2], in_=x_tm[rb * P:(rb + 1) * P, :]),
            wait=w, inc=[(dXT[rb % 2], 16)])
        xtm2_ld[rb] = a["dXT%d" % (rb % 2)]

        if rb < 2:
            ab = ffn_pre[rb]
        else:
            ab = emit_ffn(rb)
        for mc in range(C):
            for half in range(2):
                op("T", lambda e, rb=rb, mc=mc, half=half, ab=ab: e.matmul(
                    ab[half][:, 0:512],
                    vq_c(mc)[:, rb * P:(rb + 1) * P],
                    w_ck(wk_sb, mc, half * 512, 512),
                    start=False, stop=(mc == C - 1)),
                    wait=([(sS, s_wtil),
                           (sV, nrm_state["vqr"][NQ - 1])]
                          if mc == 0 and half == 0 else []),
                    inc=([(sT, 1)] if mc == C - 1 and half == 1 else []))
        t_rb = sT.v
        for half in range(2):
            op("V", lambda e, rb=rb, half=half, ab=ab: e.scalar_tensor_tensor(
                out_buf[rb % 2][:, half * 512: half * 512 + 512],
                ab[half][:, 0:512], scal[:, 1:2],
                xtm_buf[rb % 2][:, half * 512: half * 512 + 512],
                ALU.mult, ALU.add),
                wait=[(sT, t_rb), (dXT[rb % 2], xtm2_ld[rb]),
                      (dSC, g_scal)] +
                     ([(dOU[rb % 2], out_st[rb - 2])] if rb >= 2 else []),
                inc=[(sV, 1)])
        ep_done[rb] = sV.v
        a = op("Y", lambda e, rb=rb: e.dma_start(
            out=out_tm[rb * P:(rb + 1) * P, :],
            in_=out_buf[rb % 2][:, 0:D]),
            wait=[(sV, ep_done[rb])], inc=[(dOU[rb % 2], 16)])
        out_st[rb] = a["dOU%d" % (rb % 2)]

    if debug:
        dumps = [("xb", xb[:, :]), ("vq", vq[:, :]),
                 ("qw", wv_sb[:, :]), ("wt", wk_sb[:, :]),
                 ("zs", zs[:, :]), ("a_sb", a_sb[:, :]),
                 ("qs", qs[:, :]), ("ks", ks[:, :]),
                 ("scal", scal[:, 0:3]), ("sc1", sc1[0:1, 0:8]),

                 ("xfd", xf[:, :])]
        for nm, t in dumps:
            sh = [t.shape[0] if hasattr(t, 'shape') else P]
            ap = t[:, :] if not isinstance(t, bass.AP) else t
            shp = list(ap.shape)
            dt_ = ap.dtype
            dout = nc.dram_tensor("dbg_" + nm, shp, dt_,
                                  kind="ExternalOutput")
            op("Y", lambda e, dout=dout, ap=ap: e.dma_start(
                out=dout[:, :], in_=ap),
                wait=[(sV, sV.v), (sS, sS.v), (sT, sT.v)],
                inc=[(dOU[0], 16)])
    B.wait("Y", dOU[0], dOU[0].v)
    B.wait("Y", dOU[1], dOU[1].v)
    B.wait("G", dZ, dZ.v)
    B.wait("G", dMX, dMX.v)

    with nc.Block() as block:
        @block.sync
        def _(e):
            for f in B.streams["Y"]:
                f(e)

        @block.gpsimd
        def _(e):
            for f in B.streams["G"]:
                f(e)

        @block.tensor
        def _(e):
            for f in B.streams["T"]:
                f(e)

        @block.vector
        def _(e):
            for f in B.streams["V"]:
                f(e)

        @block.scalar
        def _(e):
            for f in B.streams["S"]:
                f(e)

    stack.close()
    return nc


def _prep_inputs(x, Wq, Wk, Wv, Wo, Wf, bias, n_cores, R):
    xflat = np.ascontiguousarray(np.asarray(x, np.float32).reshape(-1, D))
    wq = np.ascontiguousarray(np.asarray(Wq, np.float32).T)
    wk = np.ascontiguousarray(np.asarray(Wk, np.float32).T)
    wv = np.ascontiguousarray(np.asarray(Wv, np.float32).T)
    wo = np.ascontiguousarray(np.asarray(Wo, np.float32).T)
    wf = np.ascontiguousarray(np.asarray(Wf, np.float32).T)
    br = np.ascontiguousarray(np.asarray(bias, np.float32).reshape(1, D))
    eye = np.eye(P, dtype=np.float32)
    maps = []
    for c in range(n_cores):
        rows = xflat[c * R:(c + 1) * R]
        maps.append({
            "xT": np.ascontiguousarray(rows.T),
            "x_tm": np.ascontiguousarray(rows),
            "wqT": wq, "wkT": wk, "wvT": wv, "woT": wo, "wfT": wf,
            "bias_row": br, "ident_in": eye,
        })
    return maps


def kernel(L0_attn_x, L0_attn_Wq, L0_attn_Wk, L0_attn_Wv, L0_attn_Wo,
           L0_ffn_weight, L0_ffn_bias):
    n_cores = 8
    R = 2048
    x = np.asarray(L0_attn_x, np.float32)
    maps = _prep_inputs(x, L0_attn_Wq, L0_attn_Wk, L0_attn_Wv, L0_attn_Wo,
                        L0_ffn_weight, L0_ffn_bias, n_cores, R)
    z_groups = [[2 * b, 2 * b + 1] for b in range(n_cores // 2)]
    all_group = [list(range(n_cores))]
    nc = build_nc(n_cores, R, z_groups, all_group)
    res = run_bass_kernel_spmd(nc, maps, core_ids=list(range(n_cores)))
    out = np.empty((n_cores * R, D), np.float32)
    for c in range(n_cores):
        out[c * R:(c + 1) * R] = np.asarray(res.results[c]["out_tm"])
    return out.reshape(x.shape)


# version 22
# speedup vs baseline: 1.1386x; 1.0272x over previous
import numpy as np
import contextlib
import concourse.bass as bass
import concourse.mybir as mybir
from concourse.bass_utils import run_bass_kernel_spmd


F32 = mybir.dt.float32
BF16 = mybir.dt.bfloat16
AF = mybir.ActivationFunctionType
ALU = mybir.AluOpType
AX = mybir.AxisListType

P = 128
D = 1024
C = D // P
H = 16
MAGIC = float(np.float32(12582912.0))
EPS = 1e-5


class _Sem:
    def __init__(self, name):
        self.name = name
        self.h = None
        self.v = 0


class _Builder:
    ENGINES = ("T", "V", "S", "G", "Y")

    def __init__(self):
        self.streams = {e: [] for e in self.ENGINES}
        self.lastwait = {e: {} for e in self.ENGINES}

    def op(self, eng, fn, wait=(), inc=()):
        w = []
        for sem, val in wait:
            if val > self.lastwait[eng].get(sem.name, -1):
                w.append((sem, val))
                self.lastwait[eng][sem.name] = val
        for sem, amt in inc:
            sem.v += amt

        def run(e, fn=fn, w=tuple(w), inc=tuple(inc)):
            for sem, val in w:
                e.wait_ge(sem.h, val)
            ins = fn(e)
            for sem, amt in inc:
                ins.then_inc(sem.h, amt)
        self.streams[eng].append(run)
        return {sem.name: sem.v for sem, _ in inc}

    def wait(self, eng, sem, val):
        if val > self.lastwait[eng].get(sem.name, -1):
            self.lastwait[eng][sem.name] = val

            def run(e, sem=sem, val=val):
                e.wait_ge(sem.h, val)
            self.streams[eng].append(run)


def build_nc(n_cores, R, z_groups, all_group, debug=False):
    RB = R // P
    RGS = 512
    RG = R // RGS
    NQ = RG

    nc = bass.Bass("TRN2", target_bir_lowering=False)

    xT = nc.dram_tensor("xT", [D, R], F32, kind="ExternalInput")
    x_tm = nc.dram_tensor("x_tm", [R, D], F32, kind="ExternalInput")
    wqT = nc.dram_tensor("wqT", [D, D], F32, kind="ExternalInput")
    wkT = nc.dram_tensor("wkT", [D, D], F32, kind="ExternalInput")
    wvT = nc.dram_tensor("wvT", [D, D], F32, kind="ExternalInput")
    woT = nc.dram_tensor("woT", [D, D], F32, kind="ExternalInput")
    wfT = nc.dram_tensor("wfT", [D, D], F32, kind="ExternalInput")
    bias_row = nc.dram_tensor("bias_row", [1, D], F32, kind="ExternalInput")
    out_tm = nc.dram_tensor("out_tm", [R, D], F32, kind="ExternalOutput")

    z_in = nc.dram_tensor("z_in", [P, 512], F32, kind="Internal")
    z_out = nc.dram_tensor("z_out", [P, 512], F32, kind="Internal")
    mx_in = nc.dram_tensor("mx_in", [1, 8], F32, kind="Internal")
    mx_out = nc.dram_tensor("mx_out", [1, 8], F32, kind="Internal")
    bcst_d = nc.dram_tensor("bcst_d", [8, 512], BF16, kind="Internal")
    pcol_d = nc.dram_tensor("pcol_d", [2, P], F32, kind="Internal")
    sc_d = nc.dram_tensor("sc_d", [1, 8], F32, kind="Internal")
    ident_in = nc.dram_tensor("ident_in", [P, P], F32, kind="ExternalInput")

    stack = contextlib.ExitStack()
    sb = lambda name, shape, dt: stack.enter_context(
        nc.sbuf_tensor(name, shape, dt))
    pst = lambda name, shape, dt: stack.enter_context(
        nc.psum_tensor(name, shape, dt))

    xf = sb("xf", [P, C * R], F32)
    xb = sb("xb", [P, C * R], BF16)
    vq = sb("vq", [P, C * R], BF16)
    wv_sb = sb("wv_sb", [P, C * D], BF16)
    wq_sb = sb("wq_sb", [P, C * D], BF16)
    wk_sb = sb("wk_sb", [P, C * D], BF16)
    qw_sb = wv_sb
    qs = sb("qs", [P, D], BF16)
    ks = sb("ks", [P, D], BF16)
    zs = sb("zs", [P, 512], F32)
    a_sb = sb("a_sb", [P, 512], BF16)
    aT_sb = sb("aT_sb", [P, 256], BF16)
    bdp = sb("bdp", [P, P], BF16)
    identb = sb("identb", [P, P], BF16)
    stq = sb("stq", [P, 4], F32)
    sqs = sb("sqs", [P, 2 * 512], BF16)
    srow = sb("srow", [1, 1536], F32)
    onesb = sb("onesb", [P, 1], BF16)
    bias_sb = sb("bias_sb", [1, D], F32)
    biasa_bf = sb("biasa_bf", [1, D], BF16)
    mr_bb = sb("mr_bb", [P, 2 * RGS], BF16)
    m_bb = mr_bb[:, 0:RGS]
    r_bb = mr_bb[:, RGS:2 * RGS]
    cm_bb = sb("cm_bb", [P, RGS], BF16)
    cr_bb = sb("cr_bb", [P, RGS], BF16)
    scal = sb("scal", [P, 8], F32)
    sc1 = sb("sc1", [1, 16], F32)
    shf = sb("shf", [P, 2], F32)
    prow = sb("prow", [1, 2 * P], F32)
    ones_r = sb("ones_r", [1, P], BF16)
    red_w = sb("red_w", [P, 16], F32)
    cst = sb("cst", [P, 8], F32)
    wfs = sb("wfs", [P, D], F32)
    if R >= 2 * D:
        xtm_buf = [xf[:, (2 * i) * R + R // 2: (2 * i) * R + R // 2 + D]
                   for i in (0, 1)]
        out_buf = [xf[:, (2 * i + 1) * R + R // 2:
                      (2 * i + 1) * R + R // 2 + D] for i in (0, 1)]
    else:
        xtm_t = sb("xtm_t", [P, 2 * D], F32)
        outb_t = sb("outb_t", [P, 2 * D], F32)
        xtm_buf = [xtm_t[:, 0:D], xtm_t[:, D:2 * D]]
        out_buf = [outb_t[:, 0:D], outb_t[:, D:2 * D]]

    pb = [pst("pb%d" % i, [P, 512], F32) for i in range(8)]

    dY = _Sem("dY"); dS = _Sem("dS")
    dXL = [_Sem("dXL%d" % i) for i in range(8)]
    dQR = _Sem("dQR")
    dWV = _Sem("dWV"); dWQ = _Sem("dWQ"); dWK = _Sem("dWK")
    dWF = [_Sem("dWFa"), _Sem("dWFb")]
    dWO = _Sem("dWO"); dBI = _Sem("dBI")
    dZ = _Sem("dZ"); dMX = _Sem("dMX")
    dXT = [_Sem("dXT0"), _Sem("dXT1")]
    dOU = [_Sem("dOU0"), _Sem("dOU1")]
    sT = _Sem("sT"); sV = _Sem("sV"); sS = _Sem("sS")
    sG = _Sem("sG")
    cC = _Sem("cC")
    dBC = _Sem("dBC"); dPC = _Sem("dPC"); dSC = _Sem("dSC")
    dCR = _Sem("dCR")
    dID = _Sem("dID")
    for s in (dY, dS, dXL[0], dXL[1], dXL[2], dXL[3],
              dXL[4], dXL[5], dXL[6], dXL[7], dQR,
              dWV, dWQ, dWK, dWF[0], dWF[1], dWO, dBI,
              dZ, dMX, dXT[0], dXT[1], dOU[0], dOU[1],
              sT, sV, sS, sG, cC, dBC, dPC, dSC, dCR, dID):
        s.h = stack.enter_context(nc.semaphore(s.name))

    nc.const_aps.register(F32, MAGIC, cst[:, 1:2])
    nc.const_aps.register(F32, -1.0, cst[:, 4:5])
    nc.const_aps.register(F32, -MAGIC, cst[:, 5:6])

    B = _Builder()
    op = B.op

    def xf_c(c):
        return xf[:, c * R:(c + 1) * R]

    def xb_c(c):
        return xb[:, c * R:(c + 1) * R]

    def vq_c(c):
        return vq[:, c * R:(c + 1) * R]

    def qin_c(c):
        return xf[:, c * R:c * R + R // 2].bitcast(BF16)

    def w_ck(w, dc, lo, n):
        return w[:, dc * D + lo: dc * D + lo + n]

    def w_load(dst, src):
        return lambda e: e.dma_start(
            out=dst[:, :],
            in_=bass.AP(src, 0, [[D, P], [P * D, C], [1, D]]))

    _xq = {0: "Y", 3: "Y", 6: "Y", 1: "S", 4: "S", 7: "S",
           2: "G", 5: "G"}
    for c in (0, 1, 2, 3, 4, 5, 6, 7):
        op(_xq[c], lambda e, c=c: e.dma_start(
            out=xf_c(c), in_=xT[c * P:(c + 1) * P, :]),
            inc=[(dXL[c], 16)])
    xld = {c: (dXL[c], 16) for c in range(C)}

    for j, val in enumerate((EPS, MAGIC, MAGIC + 1.0, 2.0, -1.0, -MAGIC)):
        op("V", lambda e, j=j, val=val: e.memset(cst[:, j:j + 1], val),
           inc=([(sV, 1)] if j == 5 else []))

    op("G", lambda e: e.dma_start(out=identb[:, :], in_=ident_in[:, :]),
       inc=[(dID, 16)])
    g_ident = 16
    op("V", lambda e: e.memset(onesb[:, :], 1.0))
    a = op("V", lambda e: e.memset(ones_r[:, :], 1.0), inc=[(sV, 1)])
    v_ones = sV.v

    op("G", lambda e: e.dma_start(out=bias_sb[:, :], in_=bias_row[:, :]),
       inc=[(dBI, 16)])
    g_bias = (dBI, 16)
    op("G", w_load(wv_sb, wvT), inc=[(dWV, 16)])
    g_wv = (dWV, 16)

    cast_done = {}
    for c in range(C):
        a = op("V", lambda e, c=c: e.tensor_copy(xb_c(c), xf_c(c)),
               wait=[xld[c]], inc=[(sV, 1)])
        cast_done[c] = sV.v
    v_cast_all = sV.v

    corr_done = {}
    corr_sub = {}
    vp_rg_evac = {}
    vp_rg_tdone = {}
    g_wq = g_wk = None
    wf_ld = {}
    wf1_red = {}

    def wfh(u):
        return wfs[:, (u % 2) * 512:(u % 2) * 512 + 512]

    def wf_src(u):
        c, half = u // 2, u % 2
        return wfT[c * P:(c + 1) * P, half * 512: half * 512 + 512]

    def emit_wf1_pair2(k):
        for u in (2 * k, 2 * k + 1):
            w = [(sV, wf1_red[u - 2])] if u >= 2 else []
            a = op("Y", lambda e, u=u: e.dma_start(
                out=wfh(u), in_=wf_src(u)),
                wait=w, inc=[(dWF[u % 2], 16)])
            wf_ld[u] = a["dWF%s" % ("ab"[u % 2])]
        for u in (2 * k, 2 * k + 1):
            op("V", lambda e, u=u: e.tensor_reduce(
                red_w[:, u:u + 1], wfh(u),
                AX.X, ALU.add, apply_absolute_value=True),
                wait=[(dWF[u % 2], wf_ld[u])], inc=[(sV, 1)])
            wf1_red[u] = sV.v
    quad_gbco = {}
    sq_done = {}
    sq_used = {}
    stat_done = {}
    srow_rd = {}

    def emit_stats_mms(q):
        for c in range(C):
            ping = (q * C + c) % 2
            vw = [xld[c], (sV, cast_done[c])]
            if (q * C + c) >= 2:
                vw.append((sT, sq_used[(q * C + c) - 2]))
            op("S", lambda e, c=c, q=q, ping=ping: e.activation(
                sqs[:, ping * 512:(ping + 1) * 512],
                xb_c(c)[:, q * RGS: q * RGS + RGS], AF.Square),
                wait=vw, inc=[(sS, 1)])
            sq_done[q * C + c] = sS.v
            tw = [(sS, sq_done[q * C + c]), (sV, v_ones)]
            if c == 0 and q >= 1:
                tw.append((sV, srow_rd[q - 1]))
            op("T", lambda e, c=c, q=q: e.matmul(
                pb[6][0:1, 0:512], onesb[:, :],
                xb_c(c)[:, q * RGS: q * RGS + RGS],
                start=(c == 0), stop=(c == C - 1)),
                wait=tw)
            op("T", lambda e, c=c, ping=ping: e.matmul(
                pb[7][0:1, 0:512], onesb[:, :],
                sqs[:, ping * 512:(ping + 1) * 512],
                start=(c == 0), stop=(c == C - 1)),
                inc=[(sT, 1)])
            sq_used[q * C + c] = sT.v
        stat_done[q] = sT.v

    def emit_stats_post(q):
        t_stat = stat_done[q]
        op("V", lambda e: e.tensor_scalar(
            srow[0:1, 0:512], pb[6][0:1, 0:512], 1.0 / D, None, ALU.mult),
            wait=[(sT, t_stat)], inc=[(sV, 1)])
        op("V", lambda e: e.tensor_scalar(
            srow[0:1, 512:1024], pb[7][0:1, 0:512], 1.0 / D, None,
            ALU.mult), inc=[(sV, 1)])
        srow_rd[q] = sV.v
        B.wait("V", sV, sV.v)
        op("V", lambda e: e.tensor_tensor(
            srow[0:1, 1024:1536], srow[0:1, 0:512], srow[0:1, 0:512],
            ALU.mult), inc=[(sV, 1)])
        B.wait("V", sV, sV.v)
        op("V", lambda e: e.tensor_tensor(
            srow[0:1, 512:1024], srow[0:1, 512:1024], srow[0:1, 1024:1536],
            ALU.subtract), inc=[(sV, 1)])
        v_var = sV.v
        op("S", lambda e: e.activation(
            srow[0:1, 1024:1536], srow[0:1, 512:1024], AF.Ln,
            bias=cst[0:1, 0:1]), wait=[(sV, v_var)], inc=[(sS, 1)])
        B.wait("S", sS, sS.v)
        a = op("S", lambda e: e.activation(
            srow[0:1, 1024:1536], srow[0:1, 1024:1536], AF.Exp,
            scale=-0.5), inc=[(sS, 1)])
        s_rstd = sS.v
        gw = [(sS, s_rstd)]
        if q >= 1:
            gw.append((sV, corr_sub[q - 1]))
        a = op("G", lambda e, q=q: e.dma_start(
            out=bcst_d[q:q + 1, :], in_=srow[0:1, 0:512]),
            wait=gw, inc=[(dBC, 16)])
        a = op("G", lambda e, q=q: e.dma_start(
            out=bcst_d[4 + q:5 + q, :], in_=srow[0:1, 1024:1536]),
            inc=[(dBC, 16)])
        quad_gbco[q] = a["dBC"]
        a = op("G", lambda e, q=q: e.dma_start(
            out=cm_bb[:, :],
            in_=bass.AP(bcst_d, q * 512, [[0, P], [1, 512]])),
            wait=[(dBC, quad_gbco[q])], inc=[(dBC, 16)])
        g_corr = a["dBC"]
        for c in range(C):
            op("V", lambda e, c=c, q=q: e.tensor_tensor(
                xb_c(c)[:, q * RGS: q * RGS + RGS],
                xb_c(c)[:, q * RGS: q * RGS + RGS], cm_bb[:, :],
                ALU.subtract),
                wait=[(dBC, g_corr), (sV, cast_done[c])], inc=[(sV, 1)])
        corr_sub[q] = sV.v

    emit_stats_mms(0)
    emit_stats_post(0)
    if NQ > 1:
        emit_stats_mms(1)
        emit_stats_post(1)
        emit_wf1_pair2(0)
    op("G", w_load(wq_sb, wqT), inc=[(dWQ, 16)])
    g_wq = (dWQ, 16)
    op("G", w_load(wk_sb, wkT), inc=[(dWK, 16)])
    g_wk = (dWK, 16)
    vp_evac = {}
    vp_bank_last = {}
    for rg in range(RG):
        q = rg

        for ocb in range(C):
            bank = pb[ocb % 6]
            w0 = [(sV, corr_sub[q]), g_wv]
            if (ocb % 6) in vp_bank_last:
                w0.append((sS, vp_bank_last[ocb % 6]))
            for dc in range(C):
                op("T", lambda e, rg=rg, ocb=ocb, dc=dc, bank=bank: e.matmul(
                    bank[:, 0:512],
                    wv_sb[:, dc * D + ocb * P: dc * D + (ocb + 1) * P],
                    xb_c(dc)[:, rg * RGS: rg * RGS + RGS],
                    start=(dc == 0), stop=(dc == C - 1)),
                    wait=(w0 if dc == 0 else []),
                    inc=([(sT, 1)] if dc == C - 1 else []))
            t_g = sT.v
            sw = [(sT, t_g)]

            a = op("S", lambda e, rg=rg, ocb=ocb, bank=bank: e.copy(
                vq_c(ocb)[:, rg * RGS: rg * RGS + RGS], bank[:, 0:512]),
                wait=sw, inc=[(sS, 1)])
            vp_evac[(rg, ocb)] = sS.v
            vp_bank_last[ocb % 6] = sS.v
        vp_rg_evac[rg] = sS.v
        vp_rg_tdone[rg] = sT.v
        if rg + 2 < NQ:
            emit_stats_mms(rg + 2)
            emit_stats_post(rg + 2)
            emit_wf1_pair2(rg + 1)
    vp_last = sS.v
    t_vp_last = sT.v

    for q in range(NQ):
        rw = [(dBC, quad_gbco[q])]
        if q >= 1:
            rw.append((sV, corr_done[q - 1][1]))
        op("G", lambda e, q=q: e.dma_start(
            out=cr_bb[:, :],
            in_=bass.AP(bcst_d, (4 + q) * 512, [[0, P], [1, 512]])),
            wait=rw, inc=[(dCR, 16)])
        for c in range(C):
            cw = [] if c else [(dCR, 16 * (q + 1)), (sV, corr_sub[q]),
                               (sT, vp_rg_tdone[q])]
            op("V", lambda e, c=c, q=q: e.tensor_tensor(
                xb_c(c)[:, q * RGS: q * RGS + RGS],
                xb_c(c)[:, q * RGS: q * RGS + RGS], cr_bb[:, :],
                ALU.mult), wait=cw, inc=[(sV, 1)])
        corr_done[q] = (sV, sV.v)

    nrm_state = {"done": {}, "amax_parts": 0, "vqr": {}}

    def emit_norm_quad(q, part):
        if part == 0:
            base = 3 * NQ * 16 + 32 * q
            a = op("G", lambda e, q=q: e.dma_start(
                out=m_bb[:, :],
                in_=bass.AP(bcst_d, q * 512, [[0, P], [1, 512]])),
                wait=[(dBC, base),
                      (sV, nrm_state["done"].get(q - 1, 0)),
                      (sV, nrm_state["vqr"].get(q - 1, 0))],
                inc=[(dBC, 16)])
            a = op("G", lambda e, q=q: e.dma_start(
                out=r_bb[:, :],
                in_=bass.AP(bcst_d, (4 + q) * 512, [[0, P], [1, 512]])),
                inc=[(dBC, 16)])
            g_n = a["dBC"]
            for c in range(C):
                op("V", lambda e, c=c, q=q: e.tensor_tensor(
                    xf_c(c)[:, q * RGS: q * RGS + RGS],
                    xf_c(c)[:, q * RGS: q * RGS + RGS], m_bb[:, :],
                    ALU.subtract),
                    wait=[(dBC, g_n), (sV, cast_done[c])], inc=[(sV, 1)])
            B.wait("V", sV, sV.v)
        else:
            for c in range(C):
                op("V", lambda e, c=c, q=q: e.tensor_tensor(
                    xf_c(c)[:, q * RGS: q * RGS + RGS],
                    xf_c(c)[:, q * RGS: q * RGS + RGS], r_bb[:, :],
                    ALU.mult), inc=[(sV, 1)])
            nrm_state["done"][q] = sV.v
        return sV.v

    def emit_absmax_part(q, h):
        for c in range(4 * h, 4 * h + 4):
            op("V", lambda e, c=c, q=q: e.tensor_reduce(
                stq[:, 0:1], xf_c(c)[:, q * RGS: q * RGS + RGS],
                AX.X, ALU.max, apply_absolute_value=True),
                wait=[(sV, nrm_state["done"][q])], inc=[(sV, 1)])
            B.wait("V", sV, sV.v)
            if nrm_state["amax_parts"] == 0:
                op("V", lambda e: e.tensor_copy(stq[:, 1:2], stq[:, 0:1]),
                   inc=[(sV, 1)])
            else:
                op("V", lambda e: e.tensor_max(stq[:, 1:2], stq[:, 1:2],
                                               stq[:, 0:1]), inc=[(sV, 1)])
            nrm_state["amax_parts"] += 1
            B.wait("V", sV, sV.v)
        return sV.v

    mxs = {"g_mx": None, "staged": False}

    def _mx1():
        emit_mx_stage1()
        mxs["staged"] = True

    def _mx2():
        mxs["g_mx"] = emit_mx_stage2()

    def emit_vq_rstd(q):
        w = [(sS, vp_rg_evac[q])]
        for ocb in range(C):
            op("V", lambda e, ocb=ocb, q=q: e.tensor_tensor(
                vq_c(ocb)[:, q * RGS: q * RGS + RGS],
                vq_c(ocb)[:, q * RGS: q * RGS + RGS], r_bb[:, :],
                ALU.mult),
                wait=(w if ocb == 0 else []), inc=[(sV, 1)])
        nrm_state["vqr"][q] = sV.v

    slot_items = []
    for q in range(NQ):
        slot_items.append((5.5, lambda q=q: emit_norm_quad(q, 0), False))
        slot_items.append((5.5, lambda q=q: emit_norm_quad(q, 1), False))
        slot_items.append((2.6, lambda q=q: emit_vq_rstd(q), False))
        for h in range(2):
            slot_items.append(
                (3.3, lambda q=q, h=h: emit_absmax_part(q, h), False))
    slot_items.append((2.5, _mx1, True))
    slot_items.append((2.5, _mx2, True))
    slot_cursor = 0

    tail_budget = 42.0
    tail_used = 0.0

    def _tail_ok():
        return (slot_cursor < len(slot_items)
                and not slot_items[slot_cursor][2]
                and tail_used + slot_items[slot_cursor][0] <= tail_budget)
    while len(wf1_red) < 2 * C:
        emit_wf1_pair2(len(wf1_red) // 2)
        if _tail_ok():
            tail_used += slot_items[slot_cursor][0]
            slot_items[slot_cursor][1]()
            slot_cursor += 1
    B.wait("V", sV, sV.v)
    op("V", lambda e: e.tensor_reduce(shf[:, 0:1], red_w[:, 0:16], AX.X,
                                      ALU.add), inc=[(sV, 1)])
    v_wtot = sV.v
    a = op("G", lambda e: e.dma_start(out=pcol_d[1:2, :], in_=shf[:, 0:1]),
           wait=[(sV, v_wtot)], inc=[(dPC, 16)])
    a = op("G", lambda e: e.dma_start(out=prow[0:1, P:2 * P],
                                      in_=pcol_d[1:2, :]),
           wait=[(dPC, 16)], inc=[(dPC, 16)])
    while _tail_ok():
        tail_used += slot_items[slot_cursor][0]
        slot_items[slot_cursor][1]()
        slot_cursor += 1
    def emit_winv_stageB():
        a = op("V", lambda e: e.tensor_reduce(
            sc1[0:1, 1:2], prow[0:1, P:2 * P], AX.X, ALU.add),
            wait=[(dPC, 32)], inc=[(sV, 1)])
        B.wait("V", sV, sV.v)
        a = op("V", lambda e: e.tensor_scalar(
            sc1[0:1, 2:3], sc1[0:1, 1:2], 1.0 / (D * D), None, ALU.mult),
            inc=[(sV, 1)])
        B.wait("V", sV, sV.v)
        op("V", lambda e: e.reciprocal(sc1[0:1, 3:4], sc1[0:1, 2:3]),
           inc=[(sV, 1)])
        v_winv = sV.v
        a = op("Y", lambda e: e.dma_start(out=sc_d[0:1, 2:3],
                                          in_=sc1[0:1, 3:4]),
               wait=[(sV, v_winv)], inc=[(dSC, 16)])
        a = op("Y", lambda e: e.dma_start(
            out=scal[:, 2:3], in_=bass.AP(sc_d, 2, [[0, P], [1, 1]])),
            wait=[(dSC, 16)], inc=[(dSC, 16)])
    g_winv = 32

    def emit_mx_stage1():
        op("V", lambda e: e.tensor_copy(shf[:, 0:1], stq[:, 1:2]),
           wait=[(dPC, 32)], inc=[(sV, 1)])
        a = op("G", lambda e: e.dma_start(out=pcol_d[0:1, :],
                                          in_=shf[:, 0:1]),
               wait=[(sV, sV.v)], inc=[(dPC, 16)])
        a = op("G", lambda e: e.dma_start(out=prow[0:1, 0:P],
                                          in_=pcol_d[0:1, :]),
               wait=[(dPC, 48)], inc=[(dPC, 16)])

    def emit_mx_stage2():
        a = op("V", lambda e: e.tensor_reduce(
            shf[0:1, 1:2], prow[0:1, 0:P], AX.X, ALU.max),
            wait=[(dPC, 64)], inc=[(sV, 1)])
        B.wait("V", sV, sV.v)
        op("V", lambda e: e.memset(sc1[0:1, 8:16], 0.0), inc=[(sV, 1)])
        B.wait("V", sV, sV.v)
        a = op("V", lambda e: e.tensor_scalar(
            sc1[0:1, 8:16], sc1[0:1, 8:16], shf[0:1, 1:2], None, ALU.add),
            inc=[(sV, 1)])
        v_mx8 = sV.v
        a = op("G", lambda e: e.dma_start(
            out=mx_in[0:1, 0:8], in_=sc1[0:1, 8:16]),
            wait=[(sV, v_mx8)], inc=[(dMX, 16)])
        a = op("G", lambda e: e.collective_compute(
            "AllReduce", ALU.max, replica_groups=all_group,
            ins=[mx_in.ap().opt()], outs=[mx_out.ap().opt()]),
            wait=[(dMX, 16)], inc=[(cC, 1)])
        a = op("G", lambda e: e.dma_start(out=sc1[0:1, 0:1],
                                          in_=mx_out[0:1, 0:1]),
               wait=[(cC, cC.v)], inc=[(dMX, 16)])
        return (dMX, 32)

    wf2_ld = {}
    wf2_use = {}

    def emit_wf2_load(u):
        w = [(sV, v_wtot)] if u < 2 else [(sS, wf2_use[u - 2])]
        a = op("Y", lambda e, u=u: e.dma_start(
            out=wfh(u), in_=wf_src(u)), wait=w, inc=[(dWF[u % 2], 16)])
        wf2_ld[u] = a["dWF%s" % ("ab"[u % 2])]

    def emit_qw_slot(c, half):
        u = 2 * c + half
        if u not in wf2_ld:
            emit_wf2_load(u)
        if u + 1 not in wf2_ld and u + 1 < 2 * C:
            emit_wf2_load(u + 1)
        w = [(dSC, g_winv), (sS, sS.v), (dWF[u % 2], wf2_ld[u]),
             (sT, t_vp_last)]
        op("S", lambda e, u=u: e.activation(
            pb[6][:, 0:512],
            wfh(u),
            AF.Copy, bias=MAGIC, scale=scal[:, 2:3]),
            wait=w + [(sV, v_wtot)], inc=[(sS, 1)])
        wf2_use[u] = sS.v
        B.wait("S", sS, sS.v)
        op("S", lambda e: e.activation(pb[7][:, 0:512], pb[6][:, 0:512],
                                       AF.Relu, bias=cst[:, 2:3],
                                       scale=-1.0), inc=[(sS, 1)])
        B.wait("S", sS, sS.v)
        op("S", lambda e: e.activation(pb[6][:, 0:512], pb[7][:, 0:512],
                                       AF.Relu, bias=cst[:, 3:4], scale=-1.0),
           inc=[(sS, 1)])
        B.wait("S", sS, sS.v)
        op("S", lambda e, c=c, half=half: e.activation(
            qw_sb[:, c * D + half * 512: c * D + half * 512 + 512],
            pb[6][:, 0:512], AF.Copy, bias=-1.0), inc=[(sS, 1)])
        return sS.v

    def emit_quant_chunk(c, g_scal_val, eng):
        nd = nrm_state["done"][NQ - 1]
        if eng == "S":
            op("S", lambda e, c=c: e.activation(
                xf_c(c), xf_c(c), AF.Copy, bias=MAGIC, scale=scal[:, 0:1]),
                wait=[(sV, nd), (dSC, g_scal_val)], inc=[(sS, 1)])
            s1 = (sS, sS.v)
        else:
            op("V", lambda e, c=c: e.tensor_scalar(
                xf_c(c), xf_c(c), scal[:, 0:1], MAGIC, ALU.mult, ALU.add),
                wait=[(sV, nd), (dSC, g_scal_val)], inc=[(sV, 1)])
            s1 = (sV, sV.v)
        op("S", lambda e, c=c: e.activation(
            qin_c(c), xf_c(c), AF.Copy, bias=-MAGIC),
            wait=[s1], inc=[(sS, 1)])
        return ("S", (sS, sS.v))

    def emit_sc1_chain(g_mx_w):
        a = op("V", lambda e: e.tensor_scalar(
            sc1[0:1, 4:5], sc1[0:1, 0:1], 1.0 / 127.0, None, ALU.mult),
            wait=[g_mx_w], inc=[(sV, 1)])
        B.wait("V", sV, sV.v)
        op("V", lambda e: e.reciprocal(sc1[0:1, 5:6], sc1[0:1, 4:5]),
           inc=[(sV, 1)])
        B.wait("V", sV, sV.v)
        op("V", lambda e: e.tensor_tensor(
            sc1[0:1, 6:7], sc1[0:1, 2:3], sc1[0:1, 4:5], ALU.mult),
            inc=[(sV, 1)])
        v_alpha = sV.v
        B.wait("V", sV, sV.v)
        op("V", lambda e: e.reciprocal(sc1[0:1, 7:8], sc1[0:1, 6:7]),
           wait=[g_bias], inc=[(sV, 1)])
        B.wait("V", sV, sV.v)
        op("V", lambda e: e.tensor_scalar(
            biasa_bf[0:1, :], bias_sb[0:1, :], sc1[0:1, 7:8], None,
            ALU.mult), inc=[(sV, 1)])
        vb = sV.v
        a = op("Y", lambda e: e.dma_start(
            out=sc_d[0:1, 0:1], in_=sc1[0:1, 5:6]),
            wait=[(sV, v_alpha), (dSC, g_winv)], inc=[(dSC, 16)])
        a = op("Y", lambda e: e.dma_start(
            out=sc_d[0:1, 1:2], in_=sc1[0:1, 6:7]),
            inc=[(dSC, 16)])
        a = op("Y", lambda e: e.dma_start(
            out=sc_d[0:1, 3:4], in_=sc1[0:1, 7:8]),
            wait=[(sV, vb)], inc=[(dSC, 16)])
        a = op("Y", lambda e: e.dma_start(
            out=scal[:, 0:1], in_=bass.AP(sc_d, 0, [[0, P], [1, 1]])),
            wait=[(dSC, 80)], inc=[(dSC, 16)])
        a = op("Y", lambda e: e.dma_start(
            out=scal[:, 1:2], in_=bass.AP(sc_d, 1, [[0, P], [1, 1]])),
            inc=[(dSC, 16)])
        a = op("Y", lambda e: e.dma_start(
            out=scal[:, 3:4], in_=bass.AP(sc_d, 3, [[0, P], [1, 1]])),
            inc=[(dSC, 16)])
        return vb, 128

    qw_slots = [(c, half) for c in range(C) for half in range(2)]
    qw_cursor = 0
    quant_cursor = 0
    s_qw = None
    quant_done = {}

    qk_done = {}
    qevac = {}
    kevac = {}
    zfirst = {4: True, 5: True}
    for rb in range(RB):
        q = rb // 4
        w0 = [corr_done[q], g_wq, g_wk]
        if rb == 0:
            w0.append((sS, vp_last))
        else:
            w0.append((sS, qevac[rb - 1]))
            w0.append(kevac[rb - 1])
        first = True
        for dc in range(C):
            for half in range(2):
                op("T", lambda e, rb=rb, dc=dc, half=half: e.matmul(
                    pb[half][:, 0:512],
                    xb_c(dc)[:, rb * P:(rb + 1) * P],
                    w_ck(wq_sb, dc, half * 512, 512),
                    start=(dc == 0), stop=(dc == C - 1)),
                    wait=(w0 if first else []))
                first = False
                op("T", lambda e, rb=rb, dc=dc, half=half: e.matmul(
                    pb[2 + half][:, 0:512],
                    xb_c(dc)[:, rb * P:(rb + 1) * P],
                    w_ck(wk_sb, dc, half * 512, 512),
                    start=(dc == 0), stop=(dc == C - 1)),
                    inc=([(sT, 1)] if dc == C - 1 and half == 1 else []))
        qk_done[rb] = sT.v
        sw = [(sT, qk_done[rb])]
        op("S", lambda e, rb=rb: e.copy(
            qs[:, 0:512], pb[0][:, 0:512]), wait=sw)
        op("S", lambda e, rb=rb: e.copy(
            ks[:, 0:512], pb[2][:, 0:512]), inc=[(sS, 1)])
        qevac[rb] = sS.v
        if rb < 6:
            op("S", lambda e, rb=rb: e.copy(
                qs[:, 512:1024], pb[1][:, 0:512]))
            op("S", lambda e, rb=rb: e.copy(
                ks[:, 512:1024], pb[3][:, 0:512]), inc=[(sS, 1)])
            qevac[rb] = sS.v
            kevac[rb] = (sS, sS.v)
        else:
            op("V", lambda e, rb=rb: e.tensor_copy(
                qs[:, 512:1024], pb[1][:, 0:512]), wait=sw)
            op("V", lambda e, rb=rb: e.tensor_copy(
                ks[:, 512:1024], pb[3][:, 0:512]), inc=[(sV, 1)])
            kevac[rb] = (sV, sV.v)
        for g in range(8):
            bank = pb[4 + g // 4]
            lo = (g % 4) * P
            st = zfirst[4 + g // 4]
            zfirst[4 + g // 4] = False
            zw = []
            if g == 0:
                zw = [(sS, qevac[rb]), kevac[rb]]
            op("T", lambda e, g=g, bank=bank, lo=lo, st=st, rb=rb: e.matmul(
                bank[:, lo:lo + P],
                ks[:, g * P:(g + 1) * P], qs[:, g * P:(g + 1) * P],
                start=st, stop=(rb == RB - 1 and (g % 4) == 3)),
                wait=zw, inc=([(sT, 1)] if g == 7 else []))
        used = 0.0
        if rb == 0:
            emit_winv_stageB()
            used += 2.0
        while slot_cursor < len(slot_items) and (
                used == 0.0
                or used + slot_items[slot_cursor][0] <= 8.0):
            used += slot_items[slot_cursor][0]
            slot_items[slot_cursor][1]()
            slot_cursor += 1
        if 2 <= rb < 10 and qw_cursor < len(qw_slots):
            for _ in range(2):
                if qw_cursor < len(qw_slots):
                    cc_, hh_ = qw_slots[qw_cursor]
                    s_qw = emit_qw_slot(cc_, hh_)
                    qw_cursor += 1
    t_zdone = sT.v
    while slot_cursor < len(slot_items):
        slot_items[slot_cursor][1]()
        slot_cursor += 1
    while qw_cursor < len(qw_slots):
        cc_, hh_ = qw_slots[qw_cursor]
        s_qw = emit_qw_slot(cc_, hh_)
        qw_cursor += 1

    op("G", w_load(wk_sb, woT), wait=[(sT, qk_done[RB - 1])],
       inc=[(dWO, 16)])
    g_wo = (dWO, 16)

    v_biasa, g_scal = emit_sc1_chain(mxs["g_mx"])

    for g in range(8):
        bank = pb[4 + g // 4]
        lo = (g % 4) * P
        op("V", lambda e, g=g, bank=bank, lo=lo: e.tensor_copy(
            zs[0:64, g * 64:(g + 1) * 64], bank[0:64, lo:lo + 64]),
            wait=([(sT, t_zdone)] if g == 0 else []), inc=[(sV, 1)])
        op("V", lambda e, g=g, bank=bank, lo=lo: e.tensor_copy(
            zs[64:128, g * 64:(g + 1) * 64],
            bank[64:128, lo + 64:lo + 128]), inc=[(sV, 1)])
    v_zevac = sV.v
    a = op("G", lambda e: e.dma_start(out=z_in[:, :], in_=zs[:, :]),
           wait=[(sV, v_zevac)], inc=[(dZ, 16)])
    a = op("G", lambda e: e.collective_compute(
        "AllReduce", ALU.add, replica_groups=z_groups,
        ins=[z_in.ap().opt()], outs=[z_out.ap().opt()]),
        wait=[(dZ, 16)], inc=[(cC, 1)])
    c_z = cC.v
    while quant_cursor < C:
        _, quant_done[quant_cursor] = emit_quant_chunk(
            quant_cursor, g_scal, "V" if quant_cursor % 2 == 0 else "S")
        quant_cursor += 1
    s_quant = sS.v
    v_quant = sV.v
    g_quant = sG.v
    a = op("G", lambda e: e.dma_start(out=zs[:, :], in_=z_out[:, :]),
           wait=[(cC, c_z)], inc=[(dZ, 16)])
    g_zr = (dZ, 32)

    for g in range(8):
        op("V", lambda e, g=g: e.tensor_reduce(
            red_w[:, g:g + 1], zs[:, g * 64:(g + 1) * 64], AX.X, ALU.max),
            wait=([g_zr] if g == 0 else []), inc=[(sV, 1)])
    B.wait("V", sV, sV.v)
    for g in range(8):
        op("V", lambda e, g=g: e.tensor_scalar(
            red_w[:, g:g + 1], red_w[:, g:g + 1], -0.125, None, ALU.mult),
            inc=[(sV, 1)])
    v_zmax = sV.v
    for g in range(8):
        op("S", lambda e, g=g: e.activation(
            zs[:, g * 64:(g + 1) * 64], zs[:, g * 64:(g + 1) * 64],
            AF.Exp, bias=red_w[:, g:g + 1], scale=0.125),
            wait=([(sV, v_zmax)] if g == 0 else []), inc=[(sS, 1)])
    s_exp = sS.v
    for g in range(8):
        op("V", lambda e, g=g: e.tensor_reduce(
            red_w[:, g:g + 1], zs[:, g * 64:(g + 1) * 64], AX.X, ALU.add),
            wait=([(sS, s_exp)] if g == 0 else []), inc=[(sV, 1)])
    B.wait("V", sV, sV.v)
    op("V", lambda e: e.reciprocal(red_w[:, 0:8], red_w[:, 0:8]),
       inc=[(sV, 1)])
    B.wait("V", sV, sV.v)
    for g in range(8):
        op("V", lambda e, g=g: e.tensor_scalar(
            a_sb[:, g * 64:(g + 1) * 64], zs[:, g * 64:(g + 1) * 64],
            red_w[:, g:g + 1], None, ALU.mult), inc=[(sV, 1)])
    v_soft = sV.v

    ep_done = {}
    xtm2_ld = {}
    out_st = {}

    def emit_ffn(rb, extra=()):
        fb = [pb[4 + 2 * (rb % 2)], pb[4 + 2 * (rb % 2) + 1]]
        wt = [(sS, s_qw), (sV, v_biasa),
              (sV, v_ones), (dSC, g_scal)] + list(extra)
        if rb >= 2:
            wt.append((sV, ep_done[rb - 2]))
        for half in range(2):
            op("T", lambda e, half=half, fb=fb: e.matmul(
                fb[half][:, 0:512], ones_r[0:1, :],
                biasa_bf[0:1, half * 512: half * 512 + 512],
                start=True, stop=False), wait=(wt if half == 0 else []))
        for dc in range(C):
            for half in range(2):
                op("T", lambda e, rb=rb, dc=dc, half=half, fb=fb: e.matmul(
                    fb[half][:, 0:512],
                    qin_c(dc)[:, rb * P:(rb + 1) * P],
                    w_ck(qw_sb, dc, half * 512, 512),
                    start=False, stop=False),
                    wait=([quant_done[dc]] if half == 0 else []))
        return fb


    ffn_pre = {0: emit_ffn(0, extra=((sV, v_zevac),)),
               1: emit_ffn(1, extra=((sV, v_zevac),))}

    wtil_evac = {}
    bd_built = {}
    tr_done = {}
    abd_done = {}
    wtilmm_done = {}
    for mc in range(C):
        g = mc
        vw = [(sV, v_soft)]
        if mc >= 1:
            vw.append((sT, tr_done[mc - 1]))
        op("V", lambda e: e.memset(bdp[:, :], 0.0), wait=vw, inc=[(sV, 1)])
        B.wait("V", sV, sV.v)
        op("V", lambda e, g=g: e.tensor_copy(
            bdp[0:64, 0:64], a_sb[0:64, g * 64:(g + 1) * 64]), inc=[(sV, 1)])
        op("V", lambda e, g=g: e.tensor_copy(
            bdp[64:128, 64:128], a_sb[64:128, g * 64:(g + 1) * 64]),
            inc=[(sV, 1)])
        bd_built[mc] = sV.v
        tb = pb[mc % 2]
        tw = [(sV, bd_built[mc]), (dID, g_ident)]
        if mc == 0:
            tw.append((sS, qevac[RB - 1]))
        if mc >= 2:
            tw.append((sS, abd_done[mc - 2]))
        a = op("T", lambda e, tb=tb: e.transpose(
            tb[:, 0:64].bitcast(BF16), bdp[:, :], identb[:, :]),
            wait=tw, inc=[(sT, 1)])
        tr_done[mc] = sT.v
        a = op("S", lambda e, mc=mc, tb=tb: e.copy(
            aT_sb[:, (mc % 2) * P:(mc % 2) * P + P],
            tb[:, 0:64].bitcast(BF16)),
            wait=[(sT, tr_done[mc])] +
                 ([(sT, wtilmm_done[mc - 1])] if mc >= 1 else []),
            inc=[(sS, 1)])
        abd_done[mc] = sS.v
        s_abd = sS.v
        for half in range(2):
            wt = [(sS, s_abd), g_wo]
            if mc >= 1:
                wt.append((sS, wtil_evac[mc - 1]))
            op("T", lambda e, mc=mc, half=half: e.matmul(
                pb[2 + half][:, 0:512],
                aT_sb[:, (mc % 2) * P:(mc % 2) * P + P],
                wk_sb[:, mc * D + half * 512: mc * D + half * 512 + 512],
                start=True, stop=True), wait=wt,
                inc=([(sT, 1)] if half == 1 else []))
        wtilmm_done[mc] = sT.v
        for half in range(2):
            op("S", lambda e, mc=mc, half=half: e.activation(
                wk_sb[:, mc * D + half * 512: mc * D + half * 512 + 512],
                pb[2 + half][:, 0:512], AF.Copy, scale=scal[:, 3:4]),
                wait=[(sT, wtilmm_done[mc]), (dSC, g_scal)],
                inc=([(sS, 1)] if half == 1 else []))
        wtil_evac[mc] = sS.v
    s_wtil = sS.v


    for rb in range(RB):
        w = [(sS, s_quant), (sV, v_quant), (sG, g_quant)]
        if rb >= 2:
            w.append((sV, ep_done[rb - 2]))
        a = op("Y", lambda e, rb=rb: e.dma_start(
            out=xtm_buf[rb % 2], in_=x_tm[rb * P:(rb + 1) * P, :]),
            wait=w, inc=[(dXT[rb % 2], 16)])
        xtm2_ld[rb] = a["dXT%d" % (rb % 2)]

        if rb < 2:
            ab = ffn_pre[rb]
        else:
            ab = emit_ffn(rb)
        for mc in range(C):
            for half in range(2):
                op("T", lambda e, rb=rb, mc=mc, half=half, ab=ab: e.matmul(
                    ab[half][:, 0:512],
                    vq_c(mc)[:, rb * P:(rb + 1) * P],
                    w_ck(wk_sb, mc, half * 512, 512),
                    start=False, stop=(mc == C - 1)),
                    wait=([(sS, s_wtil),
                           (sV, nrm_state["vqr"][NQ - 1])]
                          if mc == 0 and half == 0 else []),
                    inc=([(sT, 1)] if mc == C - 1 and half == 1 else []))
        t_rb = sT.v
        for half in range(2):
            op("V", lambda e, rb=rb, half=half, ab=ab: e.scalar_tensor_tensor(
                out_buf[rb % 2][:, half * 512: half * 512 + 512],
                ab[half][:, 0:512], scal[:, 1:2],
                xtm_buf[rb % 2][:, half * 512: half * 512 + 512],
                ALU.mult, ALU.add),
                wait=[(sT, t_rb), (dXT[rb % 2], xtm2_ld[rb]),
                      (dSC, g_scal)] +
                     ([(dOU[rb % 2], out_st[rb - 2])] if rb >= 2 else []),
                inc=[(sV, 1)])
        ep_done[rb] = sV.v
        a = op("Y", lambda e, rb=rb: e.dma_start(
            out=out_tm[rb * P:(rb + 1) * P, :],
            in_=out_buf[rb % 2][:, 0:D]),
            wait=[(sV, ep_done[rb])], inc=[(dOU[rb % 2], 16)])
        out_st[rb] = a["dOU%d" % (rb % 2)]

    if debug:
        dumps = [("xb", xb[:, :]), ("vq", vq[:, :]),
                 ("qw", wv_sb[:, :]), ("wt", wk_sb[:, :]),
                 ("zs", zs[:, :]), ("a_sb", a_sb[:, :]),
                 ("qs", qs[:, :]), ("ks", ks[:, :]),
                 ("scal", scal[:, 0:3]), ("sc1", sc1[0:1, 0:8]),

                 ("xfd", xf[:, :])]
        for nm, t in dumps:
            sh = [t.shape[0] if hasattr(t, 'shape') else P]
            ap = t[:, :] if not isinstance(t, bass.AP) else t
            shp = list(ap.shape)
            dt_ = ap.dtype
            dout = nc.dram_tensor("dbg_" + nm, shp, dt_,
                                  kind="ExternalOutput")
            op("Y", lambda e, dout=dout, ap=ap: e.dma_start(
                out=dout[:, :], in_=ap),
                wait=[(sV, sV.v), (sS, sS.v), (sT, sT.v)],
                inc=[(dOU[0], 16)])
    B.wait("Y", dOU[0], dOU[0].v)
    B.wait("Y", dOU[1], dOU[1].v)
    B.wait("G", dZ, dZ.v)
    B.wait("G", dMX, dMX.v)

    with nc.Block() as block:
        @block.sync
        def _(e):
            for f in B.streams["Y"]:
                f(e)

        @block.gpsimd
        def _(e):
            for f in B.streams["G"]:
                f(e)

        @block.tensor
        def _(e):
            for f in B.streams["T"]:
                f(e)

        @block.vector
        def _(e):
            for f in B.streams["V"]:
                f(e)

        @block.scalar
        def _(e):
            for f in B.streams["S"]:
                f(e)

    stack.close()
    return nc


def _prep_inputs(x, Wq, Wk, Wv, Wo, Wf, bias, n_cores, R):
    xflat = np.ascontiguousarray(np.asarray(x, np.float32).reshape(-1, D))
    wq = np.ascontiguousarray(np.asarray(Wq, np.float32).T)
    wk = np.ascontiguousarray(np.asarray(Wk, np.float32).T)
    wv = np.ascontiguousarray(np.asarray(Wv, np.float32).T)
    wo = np.ascontiguousarray(np.asarray(Wo, np.float32).T)
    wf = np.ascontiguousarray(np.asarray(Wf, np.float32).T)
    br = np.ascontiguousarray(np.asarray(bias, np.float32).reshape(1, D))
    eye = np.eye(P, dtype=np.float32)
    maps = []
    for c in range(n_cores):
        rows = xflat[c * R:(c + 1) * R]
        maps.append({
            "xT": np.ascontiguousarray(rows.T),
            "x_tm": np.ascontiguousarray(rows),
            "wqT": wq, "wkT": wk, "wvT": wv, "woT": wo, "wfT": wf,
            "bias_row": br, "ident_in": eye,
        })
    return maps


def kernel(L0_attn_x, L0_attn_Wq, L0_attn_Wk, L0_attn_Wv, L0_attn_Wo,
           L0_ffn_weight, L0_ffn_bias):
    n_cores = 8
    R = 2048
    x = np.asarray(L0_attn_x, np.float32)
    maps = _prep_inputs(x, L0_attn_Wq, L0_attn_Wk, L0_attn_Wv, L0_attn_Wo,
                        L0_ffn_weight, L0_ffn_bias, n_cores, R)
    z_groups = [[2 * b, 2 * b + 1] for b in range(n_cores // 2)]
    all_group = [list(range(n_cores))]
    nc = build_nc(n_cores, R, z_groups, all_group)
    res = run_bass_kernel_spmd(nc, maps, core_ids=list(range(n_cores)))
    out = np.empty((n_cores * R, D), np.float32)
    for c in range(n_cores):
        out[c * R:(c + 1) * R] = np.asarray(res.results[c]["out_tm"])
    return out.reshape(x.shape)
